# version 09 (final)
# speedup vs baseline: 1.2892x; 1.0103x over previous
import numpy as np
from contextlib import ExitStack

import concourse.bass as bass
import concourse.mybir as mybir
from concourse.bass_utils import run_bass_kernel_spmd

F32 = mybir.dt.float32
BF16 = mybir.dt.bfloat16
I32 = mybir.dt.int32
AF = mybir.ActivationFunctionType
ALU = mybir.AluOpType
AX = mybir.AxisListType.X

T_SEQ = 2048
D = 2048
NT = T_SEQ // 128
ND = D // 128
EPS = 1e-5
MAGIC = 12582912.0
N_PHASES_ALL = 4



class Graph:

    def __init__(self, block, sT, sV, sS, sP, qsems, gsems):
        self.block = block
        self.ctr = {"T": sT, "V": sV, "S": sS, "P": sP}
        self.qsems = {"Q": qsems, "G": gsems}
        self.qlast = {"Q": [None] * len(qsems), "G": [None] * len(gsems)}
        self.qrr = {"Q": 0, "G": 0}
        self.steps = []

    def add(self, eng, fn, deps=(), n_dma=1, predeps=()):
        sid = len(self.steps)
        deps = [d for d in deps if d is not None]
        self.predeps = getattr(self, "predeps", {})
        if predeps:
            self.predeps[sid] = list(predeps)
        if eng in ("Q", "G"):
            k = self.qrr[eng] % len(self.qsems[eng])
            self.qrr[eng] += 1
            prev = self.qlast[eng][k]
            if prev is not None:
                deps = deps + [prev]
            self.qlast[eng][k] = sid
            ctr = self.qsems[eng][k]
            mark = (ctr, ctr.n + 16 * n_dma)
            ctr.n += 16 * n_dma
            self.steps.append([eng, fn, deps, mark, k])
        else:
            ctr = self.ctr[eng]
            mark = (ctr, ctr.n + 1)
            ctr.n += 1
            self.steps.append([eng, fn, deps, mark, None])
        return sid

    def emit(self):
        sections = {"T": self.block.tensor, "V": self.block.vector,
                    "S": self.block.scalar, "Q": self.block.sync}
        for ename in ("Q", "GP", "T", "V", "S"):
            if ename == "GP":
                mysteps = [st for st in self.steps if st[0] in ("G", "P")]
                sec = self.block.gpsimd
            else:
                mysteps = [st for st in self.steps if st[0] == ename]
                sec = sections[ename]
            if not mysteps:
                continue

            def _emit(h, mysteps=mysteps):
                pd = getattr(self, "predeps", {})
                for sid, st in enumerate(self.steps):
                    if st not in mysteps:
                        continue
                    eng, fn, deps, mark, k = st
                    for pctr, pcnt in pd.get(sid, ()):
                        h.wait_ge(pctr.sem, pcnt)
                    for d in deps:
                        dctr, dcnt = self.steps[d][3]
                        h.wait_ge(dctr.sem, dcnt)
                    if eng in ("Q", "G"):
                        for ins in fn(h):
                            ins.then_inc(mark[0].sem, 16)
                    else:
                        fn(h)
                        h.drain().then_inc(mark[0].sem, 1)
            sec(_emit)


class Ctr:
    def __init__(self, sem):
        self.sem = sem
        self.n = 0

    def inc(self, instr, k=1):
        instr.then_inc(self.sem, k)
        self.n += k
        return self.n


def dec16g(h, lutb, acc, idxf, msk, w):
    for j in range(16):
        if j == 0:
            h.tensor_scalar(acc[:, 0:w], idxf[:, 0:w], 0.0, lutb[:, 0:1],
                            ALU.is_equal, ALU.mult)
        else:
            h.tensor_scalar(msk[:, 0:w], idxf[:, 0:w], float(j),
                            lutb[:, j:j + 1], ALU.is_equal, ALU.mult)
            h.drain()
            h.tensor_add(acc[:, 0:w], acc[:, 0:w], msk[:, 0:w])
        h.drain()


def emit_phase3(c):
    nc, block, barrier = c.nc, c.block, c.barrier
    sT, sV, sS, sP, sO = c.sT, c.sV, c.sS, c.sP, c.sO
    dRA = c.dRA
    idf, idb, rA, rB, y_h = c.idf, c.idb, c.rA, c.rB, c.y_h
    outdr, xndr = c.outdr, c.xndr
    out_dst3 = rA if c.n_phases > 3 else y_h
    out_ctr3 = dRA if c.n_phases > 3 else sO
    SCL = float(128.0 ** -0.5)
    NEGC = -1000000064.0

    def G_(block):
        return Graph(block, sT, sV, sS, sP, [c.dQa, c.dQb], [c.dGa, c.dGb])

    with ExitStack() as po_:
        oe = po_.enter_context
        cap = oe(nc.sbuf_tensor("cap3", [128, 4 * 2048], BF16))
        krope = oe(nc.sbuf_tensor("krope3", [128, 4 * 512], BF16))
        vg = oe(nc.sbuf_tensor("vg3", [128, 4 * 512], BF16))
        CSk = oe(nc.sbuf_tensor("CSk3", [128, 512], F32))
        SSk = oe(nc.sbuf_tensor("SSk3", [128, 512], F32))
        wstg = oe(nc.sbuf_tensor("wstg3", [128, 2048], F32))
        wbf = oe(nc.sbuf_tensor("wbf3", [128, 2048], BF16))
        w2cb = oe(nc.sbuf_tensor("w2cb3", [128, 4], BF16))
        b1c = oe(nc.sbuf_tensor("b1c3", [128, 4], F32))
        b2s = oe(nc.sbuf_tensor("b2s3", [1, 1], F32))
        iotc = oe(nc.sbuf_tensor("iotc3", [128, 16], F32))
        permf = oe(nc.sbuf_tensor("perm3", [128, 128], F32))
        mcol = oe(nc.sbuf_tensor("mcol3", [128, 16], F32))
        rcol = oe(nc.sbuf_tensor("rcol3", [128, 16], F32))
        impc = oe(nc.sbuf_tensor("impc3", [128, 16], F32))
        rankc = oe(nc.sbuf_tensor("rankc3", [128, 16], F32))
        indc = oe(nc.sbuf_tensor("indc3", [128, 16], F32))
        rpc = oe(nc.sbuf_tensor("rpc3", [128, 16], F32))
        gv = oe(nc.sbuf_tensor("gv3", [128, 8], F32))
        gcb = oe(nc.sbuf_tensor("gcb3", [128, 4], BF16))
        onec = oe(nc.sbuf_tensor("onec3", [128, 1], BF16))
        ones1 = oe(nc.sbuf_tensor("ones3", [1, 128], F32))
        cst = oe(nc.sbuf_tensor("cstp3", [128, 128], F32))
        exl = oe(nc.sbuf_tensor("exl3", [128, 4], F32))
        ptrA = oe(nc.psum_tensor("pA3", [128, 2048], F32))
        ptrB = oe(nc.psum_tensor("pB3", [128, 2048], F32))

        g0 = G_(block)
        s_c = g0.add("Q", lambda h: [
            h.dma_start(out=b1c[:], in_=c.b1c_h[:]),
            h.dma_start(out=b2s[:], in_=c.b2s_h[:]),
            h.dma_start(out=iotc[:], in_=c.iotc_h[:]),
            h.dma_start(out=permf[:], in_=c.perm_h[:]),
            h.dma_start(out=exl[:, 0:4], in_=c.w2c_h[:]),
        ], n_dma=5, predeps=[(dRA, dRA.n), (sO, sO.n)])
        s_ones = g0.add("V", lambda h: h.memset(ones1[:], 1.0))
        s_onec = g0.add("V", lambda h: h.memset(onec[:], 1.0))
        s_w2cb = g0.add("V", lambda h: h.tensor_copy(w2cb[:], exl[:, 0:4]), deps=[s_c])
        g0.emit()
        barrier()

        with ExitStack() as px_:
            xe = px_.enter_context
            xnT = xe(nc.sbuf_tensor("xnT3", [128, ND * 2048], BF16))

            with ExitStack() as p1_:
                p1 = p1_.enter_context
                rstg = p1(nc.sbuf_tensor("rstg3", [128, 2 * 2048], F32))
                xnbuf = p1(nc.sbuf_tensor("xnbuf3", [128, 2 * 2048], BF16))
                trash = p1(nc.sbuf_tensor("trash31", [128, 2048], BF16))
                g = G_(block)
                c.bg_alloc(p1)
                c.bg_pump(g, 6)
                ld = [None] * NT
                sq = [None] * NT
                rs = [None] * NT
                for tt in range(NT):
                    def fl(h, tt=tt):
                        b = (tt % 2) * 2048
                        return [h.dma_start(out=rstg[:, b:b + 2048],
                                            in_=rB[tt * 128:(tt + 1) * 128, :])]
                    ld[tt] = g.add("Q", fl, deps=[rs[tt - 2] if tt >= 2 else None,
                                                  sq[tt - 2] if tt >= 2 else None])

                    def fs(h, tt=tt):
                        b = (tt % 2) * 2048
                        h.activation(trash[:], rstg[:, b:b + 2048], AF.Square,
                                     accum_out=cst[:, 32 + tt:32 + tt + 1])
                    sq[tt] = g.add("S", fs, deps=[ld[tt], sq[tt - 1] if tt else None])

                    def fv(h, tt=tt):
                        b = (tt % 2) * 2048
                        h.reduce_sum(cst[:, tt:tt + 1], rstg[:, b:b + 2048], axis=AX)
                    rs[tt] = g.add("V", fv, deps=[ld[tt]])

                def fstat(h):
                    h.tensor_scalar_mul(mcol[:, 0:NT], cst[:, 0:NT], 1.0 / D)
                    h.tensor_scalar_mul(cst[:, 32:32 + NT], cst[:, 32:32 + NT], 1.0 / D)
                    h.drain()
                    h.tensor_tensor(cst[:, 64:64 + NT], mcol[:, 0:NT], mcol[:, 0:NT],
                                    ALU.mult)
                    h.drain()
                    h.tensor_sub(cst[:, 32:32 + NT], cst[:, 32:32 + NT],
                                 cst[:, 64:64 + NT])
                    h.drain()
                    h.tensor_scalar_add(cst[:, 32:32 + NT], cst[:, 32:32 + NT], EPS)
                s_stat = g.add("V", fstat, deps=rs + sq)

                def frstd(h):
                    h.activation(cst[:, 32:32 + NT], cst[:, 32:32 + NT], AF.Ln)
                    h.drain()
                    h.activation(rcol[:, 0:NT], cst[:, 32:32 + NT], AF.Exp, scale=-0.5)
                s_rstd = g.add("S", frstd, deps=[s_stat])

                ld2 = [None] * NT
                cast = [None] * NT
                trg = {}
                tev = {}
                xw = [None] * NT
                for tt in range(NT):
                    def fl2(h, tt=tt):
                        b = (tt % 2) * 2048
                        return [h.dma_start(out=rstg[:, b:b + 2048],
                                            in_=rB[tt * 128:(tt + 1) * 128, :])]
                    dp = [rs[NT - 2 + tt], sq[NT - 2 + tt]] if tt < 2 else [cast[tt - 2]]
                    ld2[tt] = g.add("G", fl2, deps=dp)

                    def fc(h, tt=tt):
                        b = (tt % 2) * 2048
                        h.tensor_scalar(xnbuf[:, b:b + 2048], rstg[:, b:b + 2048],
                                        mcol[:, tt:tt + 1], rcol[:, tt:tt + 1],
                                        ALU.subtract, ALU.mult)
                    cast[tt] = g.add("V", fc, deps=[ld2[tt], s_rstd,
                                                    trg.get((tt - 2, 3)),
                                                    xw[tt - 2] if tt >= 2 else None])
                    for gi in range(4):
                        def ft(h, tt=tt, gi=gi):
                            ps = ptrA if gi % 2 == 0 else ptrB
                            for j in range(4):
                                dt = gi * 4 + j
                                b = (tt % 2) * 2048
                                h.matmul(ps[:, j * 128:(j + 1) * 128],
                                         xnbuf[:, b + dt * 128: b + (dt + 1) * 128],
                                         idb[:], start=True, stop=True)
                        k = tt * 4 + gi
                        pk = (divmod(k - 2, 4)) if k >= 2 else None
                        trg[(tt, gi)] = g.add("T", ft, deps=[cast[tt],
                                                             tev.get(pk) if pk else None])

                        def fe(h, tt=tt, gi=gi):
                            ps = ptrA if gi % 2 == 0 else ptrB
                            for j in range(4):
                                dt = gi * 4 + j
                                h.tensor_copy(
                                    xnT[:, dt * 2048 + tt * 128: dt * 2048 + (tt + 1) * 128],
                                    ps[:, j * 128:(j + 1) * 128])
                        tev[(tt, gi)] = g.add("V", fe, deps=[trg[(tt, gi)]])

                    def fxw(h, tt=tt):
                        b = (tt % 2) * 2048
                        return [h.dma_start(out=xndr[tt, dt],
                                            in_=xnbuf[:, b + dt * 128: b + (dt + 1) * 128])
                                for dt in range(16)]
                    xw[tt] = g.add("G", fxw, deps=[cast[tt]], n_dma=16)
                g.emit()
                p1_marks = {"xw": [g.steps[xw[tt]][3] for tt in range(NT)]}
            barrier()

            with ExitStack() as p2_:
                p2 = p2_.enter_context
                gtr = p2(nc.sbuf_tensor("gtr3", [128, 2 * 2048], BF16))
                impB = p2(nc.sbuf_tensor("impB3", [128, 2048], F32))
                imp_row = p2(nc.sbuf_tensor("imp3", [1, 2048], F32))
                onesr = p2(nc.sbuf_tensor("onesr3", [1, 2048], F32))
                cumr = p2(nc.sbuf_tensor("cumr3", [1, 2048], F32))
                trash = p2(nc.sbuf_tensor("trash32", [128, 2048], BF16))
                g = G_(block)
                c.bg_alloc(p2)
                c.bg_pump(g, 6)
                qw = [None] * 4
                vw = [None] * 4
                tg = [None] * 4
                sg = [None] * 4
                ti = [None] * 4
                for ft_ in range(4):
                    def fq(h, ft_=ft_):
                        return [h.dma_start(out=wstg[:], in_=c.w1_h[ft_])]
                    qw[ft_] = g.add("Q", fq, deps=[vw[ft_ - 1] if ft_ else None])

                    def fv(h):
                        h.tensor_copy(wbf[:], wstg[:])
                    vw[ft_] = g.add("V", fv, deps=[qw[ft_],
                                                   tg[ft_ - 1] if ft_ else None])

                    def ftg(h):
                        for dt in range(16):
                            for nck in range(4):
                                h.matmul(ptrA[:, nck * 512:(nck + 1) * 512],
                                         wbf[:, dt * 128:(dt + 1) * 128],
                                         xnT[:, dt * 2048 + nck * 512:
                                             dt * 2048 + (nck + 1) * 512],
                                         start=(dt == 0), stop=(dt == 15))
                    tg[ft_] = g.add("T", ftg, deps=[vw[ft_],
                                                    sg[ft_ - 1] if ft_ else None])

                    def fsg(h, ft_=ft_):
                        b = (ft_ % 2) * 2048
                        h.activation(gtr[:, b:b + 2048], ptrA[:], AF.Gelu,
                                     bias=b1c[:, ft_:ft_ + 1])
                    sg[ft_] = g.add("S", fsg, deps=[tg[ft_],
                                                    ti[ft_ - 2] if ft_ >= 2 else None])

                    def fti(h, ft_=ft_):
                        b = (ft_ % 2) * 2048
                        for nck in range(4):
                            h.matmul(ptrB[0:1, nck * 512:(nck + 1) * 512],
                                     w2cb[:, ft_:ft_ + 1],
                                     gtr[:, b + nck * 512: b + (nck + 1) * 512],
                                     start=(ft_ == 0), stop=(ft_ == 3))
                    ti[ft_] = g.add("T", fti, deps=[sg[ft_]])

                def fimp(h):
                    h.tensor_scalar(imp_row[:], ptrB[0:1, :], b2s[0:1, 0:1], None,
                                    ALU.add)
                s_imp = g.add("V", fimp, deps=[ti[3]])

                def ftib(h):
                    for nck in range(4):
                        h.matmul(ptrA[:, nck * 512:(nck + 1) * 512], ones1[:],
                                 imp_row[0:1, nck * 512:(nck + 1) * 512],
                                 start=True, stop=True)
                t_ib = g.add("T", ftib, deps=[s_imp, sg[3]])
                s_ib = g.add("V", lambda h: h.tensor_copy(impB[:], ptrA[:]),
                             deps=[t_ib])
                tic = [None] * 16
                vic = [None] * 16
                for lt in range(16):
                    def ftc(h, lt=lt):
                        sl = (lt % 2) * 512
                        h.matmul(ptrB[:, sl:sl + 1],
                                 imp_row[0:1, lt * 128:(lt + 1) * 128],
                                 ones1[0:1, 0:1], start=True, stop=True)
                    tic[lt] = g.add("T", ftc, deps=[s_imp,
                                                    vic[lt - 2] if lt >= 2 else None])

                    def fvc(h, lt=lt):
                        sl = (lt % 2) * 512
                        h.tensor_copy(impc[:, lt:lt + 1], ptrB[:, sl:sl + 1])
                    vic[lt] = g.add("V", fvc, deps=[tic[lt]])
                vr = [None] * 16
                for lt in range(16):
                    def fr(h, lt=lt):
                        h.tensor_scalar(trash[:], impB[:], impc[:, lt:lt + 1], 0.0,
                                        ALU.is_gt, ALU.add,
                                        accum_out=rankc[:, lt:lt + 1])
                    vr[lt] = g.add("V", fr, deps=[s_ib, vic[lt],
                                                  vr[lt - 1] if lt else None])
                s_ind = g.add("V", lambda h: h.tensor_scalar(
                    indc[:], rankc[:], 512.0, None, ALU.is_lt), deps=vr)
                tir = [None] * 16
                for lt in range(16):
                    def fti2(h, lt=lt):
                        h.matmul(ptrA[0:1, lt * 128:(lt + 1) * 128],
                                 indc[:, lt:lt + 1], idf[:], start=True, stop=True)
                    tir[lt] = g.add("T", fti2, deps=[s_ind, s_ib])
                s_irow = g.add("V", lambda h: h.tensor_copy(imp_row[:], ptrA[0:1, :]),
                               deps=tir)

                def fcum(h):
                    h.memset(onesr[:], 1.0)
                    h.drain()
                    h.tensor_tensor_scan(cumr[:], onesr[:], imp_row[:], 0.0,
                                         ALU.mult, ALU.add)
                    h.drain()
                    h.tensor_scalar_add(cumr[:], cumr[:], -1.0)
                s_cum = g.add("V", fcum, deps=[s_irow])
                trp = [None] * 16
                vrp = [None] * 16
                for lt in range(16):
                    def ftrp(h, lt=lt):
                        sl = (lt % 2) * 512
                        h.matmul(ptrB[:, sl:sl + 1],
                                 cumr[0:1, lt * 128:(lt + 1) * 128],
                                 ones1[0:1, 0:1], start=True, stop=True)
                    trp[lt] = g.add("T", ftrp, deps=[s_cum,
                                                     vrp[lt - 2] if lt >= 2 else None,
                                                     vic[15]])

                    def fvrp(h, lt=lt):
                        sl = (lt % 2) * 512
                        h.tensor_copy(rpc[:, lt:lt + 1], ptrB[:, sl:sl + 1])
                    vrp[lt] = g.add("V", fvrp, deps=[trp[lt]])
                g.emit()
            barrier()

            with ExitStack() as p2b_:
                p2 = p2b_.enter_context
                iotaJ = p2(nc.sbuf_tensor("iotaJ3", [128, 512], F32))
                iotaL = p2(nc.sbuf_tensor("iotaL3", [128, 2048], F32))
                stb = p2(nc.sbuf_tensor("stb3", [128, 16 * 512], BF16))
                stf2 = p2(nc.sbuf_tensor("stf3", [128, 2 * 512], F32))
                xsel = p2(nc.sbuf_tensor("xsel3", [128, 16 * 512], BF16))
                ktcs_sb = p2(nc.sbuf_tensor("ktcs3", [128, 2048], F32))
                ktss_sb = p2(nc.sbuf_tensor("ktss3", [128, 2048], F32))
                exl2 = p2(nc.sbuf_tensor("exl23", [128, 4], F32))
                posr = p2(nc.sbuf_tensor("posr3", [1, 512], F32))
                impr = p2(nc.sbuf_tensor("impr3", [1, 512], F32))
                grow = p2(nc.sbuf_tensor("grow3", [1, 512], F32))
                xnstg = p2(nc.sbuf_tensor("xnstg3", [128, 2 * 128], BF16))
                wvstg = p2(nc.sbuf_tensor("wvstg3", [128, 2 * 512], F32))
                wvbf = p2(nc.sbuf_tensor("wvbf3", [128, 2 * 512], BF16))
                ksb = p2(nc.sbuf_tensor("ksb3", [128, 512], F32))
                bufk = p2(nc.sbuf_tensor("bufk3", [128, 2 * 512], F32))
                capf = p2(nc.sbuf_tensor("capf3", [128, 2048], F32))
                g = G_(block)
                c.bg_alloc(p2)
                c.bg_pump(g, 8)

                s_kt = g.add("Q", lambda h: [
                    h.dma_start(out=ktcs_sb[:].rearrange("p (a c) -> p a c", a=16),
                                in_=c.ktcs_h.transpose([1, 0, 2])),
                    h.dma_start(out=ktss_sb[:].rearrange("p (a c) -> p a c", a=16),
                                in_=c.ktss_h.transpose([1, 0, 2])),
                ], n_dma=2)
                s_ioj = g.add("P", lambda h: h.iota(
                    iotaJ[:], pattern=[[1, 512]], base=0, channel_multiplier=0,
                    allow_small_or_imprecise_dtypes=True))
                s_iol = g.add("P", lambda h: h.iota(
                    iotaL[:], pattern=[[1, 2048]], base=0, channel_multiplier=0,
                    allow_small_or_imprecise_dtypes=True))

                vstb = [None] * 16
                for lt in range(16):
                    def fst(h, lt=lt):
                        h.tensor_scalar(stb[:, lt * 512:(lt + 1) * 512], iotaJ[:],
                                        rpc[:, lt:lt + 1], indc[:, lt:lt + 1],
                                        ALU.is_equal, ALU.mult)
                    vstb[lt] = g.add("V", fst, deps=[s_ioj])

                tga = [None] * 16
                vsf = [None] * 16
                vex = [None] * 16
                for lt in range(16):
                    def fsf(h, lt=lt):
                        b = (lt % 2) * 512
                        h.tensor_scalar(stf2[:, b:b + 512], iotaJ[:],
                                        rpc[:, lt:lt + 1], indc[:, lt:lt + 1],
                                        ALU.is_equal, ALU.mult)
                    vsf[lt] = g.add("V", fsf, deps=[s_ioj,
                                                    tga[lt - 2] if lt >= 2 else None])

                    def fex(h, lt=lt):
                        b = (lt % 2) * 2
                        h.tensor_copy(exl2[:, b:b + 1], iotc[:, lt:lt + 1])
                        h.drain()
                        h.tensor_copy(exl2[:, b + 1:b + 2], impc[:, lt:lt + 1])
                    vex[lt] = g.add("V", fex, deps=[tga[lt - 2] if lt >= 2 else None])

                    def fga(h, lt=lt):
                        b = (lt % 2) * 512
                        be = (lt % 2) * 2
                        st_, sp_ = (lt == 0), (lt == 15)
                        h.matmul(ptrA[:, 0:512], ktcs_sb[:, lt * 128:(lt + 1) * 128],
                                 stf2[:, b:b + 512], start=st_, stop=sp_)
                        h.matmul(ptrA[:, 512:1024], ktss_sb[:, lt * 128:(lt + 1) * 128],
                                 stf2[:, b:b + 512], start=st_, stop=sp_)
                        h.matmul(ptrA[0:1, 1024:1536], exl2[:, be:be + 1],
                                 stf2[:, b:b + 512], start=st_, stop=sp_)
                        h.matmul(ptrA[0:1, 1536:2048], exl2[:, be + 1:be + 2],
                                 stf2[:, b:b + 512], start=st_, stop=sp_)
                    tga[lt] = g.add("T", fga, deps=[vsf[lt], vex[lt], s_kt])

                def fevg(h):
                    h.tensor_copy(CSk[:], ptrA[:, 0:512])
                    h.tensor_copy(SSk[:], ptrA[:, 512:1024])
                    h.tensor_copy(posr[:], ptrA[0:1, 1024:1536])
                    h.tensor_copy(impr[:], ptrA[0:1, 1536:2048])
                s_evg = g.add("V", fevg, deps=[tga[15]])
                s_sig = g.add("S", lambda h: h.activation(grow[:], impr[:],
                                                          AF.Sigmoid), deps=[s_evg])
                tgv = [None] * 8
                vgv = [None] * 8
                for i in range(8):
                    def ftv(h, i=i):
                        sl = (i % 2) * 512
                        cidx = (i % 4) * 128
                        h.matmul(ptrB[:, sl:sl + 1],
                                 (grow[0:1, cidx:cidx + 128] if i < 4
                                  else posr[0:1, cidx:cidx + 128]),
                                 ones1[0:1, 0:1], start=True, stop=True)
                    tgv[i] = g.add("T", ftv, deps=[s_sig if i < 4 else s_evg,
                                                   vgv[i - 2] if i >= 2 else None])

                    def fvv(h, i=i):
                        sl = (i % 2) * 512
                        h.tensor_copy(gv[:, i:i + 1], ptrB[:, sl:sl + 1])
                    vgv[i] = g.add("V", fvv, deps=[tgv[i]])
                s_gcb = g.add("V", lambda h: h.tensor_copy(gcb[:], gv[:, 0:4]),
                              deps=vgv[0:4])

                qxn = {}
                txs = {}
                vxs = [None] * 16
                for dt in range(16):
                    for lt in range(16):
                        k = dt * 16 + lt
                        def fq(h, dt=dt, lt=lt, k=k):
                            b = (k % 2) * 128
                            return [h.dma_start(out=xnstg[:, b:b + 128],
                                                in_=xndr[lt, dt])]
                        prev = txs.get(k - 2)
                        qxn[k] = g.add("Q", fq, deps=[prev],
                                       predeps=[p1_marks["xw"][lt]])

                        def ft(h, dt=dt, lt=lt, k=k):
                            b = (k % 2) * 128
                            h.matmul(ptrB[:, 1024:1536],
                                     xnstg[:, b:b + 128],
                                     stb[:, lt * 512:(lt + 1) * 512],
                                     start=(lt == 0), stop=(lt == 15))
                        txs[k] = g.add("T", ft, deps=[qxn[k], vstb[lt],
                                                      vxs[dt - 1] if (lt == 0 and dt >= 1)
                                                      else None])

                    def fvx(h, dt=dt):
                        h.tensor_copy(xsel[:, dt * 512:(dt + 1) * 512],
                                      ptrB[:, 1024:1536])
                    vxs[dt] = g.add("V", fvx, deps=[txs[dt * 16 + 15]])

                qwk = [None] * 4
                vwk = [None] * 4
                tkT = [None] * 4
                for ct in range(4):
                    def fq(h, ct=ct):
                        return [h.dma_start(out=wstg[:], in_=c.wkvk_h[ct])]
                    qwk[ct] = g.add("Q", fq, deps=[vwk[ct - 1] if ct else None])

                    def fv(h):
                        h.tensor_copy(wbf[:], wstg[:])
                    vwk[ct] = g.add("V", fv, deps=[qwk[ct],
                                                   tkT[ct - 1] if ct else None])

                    def ftk(h, ct=ct):
                        for dt in range(16):
                            h.matmul(ptrA[:, ct * 512:(ct + 1) * 512],
                                     wbf[:, dt * 128:(dt + 1) * 128],
                                     xsel[:, dt * 512:(dt + 1) * 512],
                                     start=(dt == 0), stop=(dt == 15))
                    tkT[ct] = g.add("T", ftk, deps=[vwk[ct], s_evg] + vxs)

                qwv = [None] * 16
                vwv = [None] * 16
                tv = {}
                for dt in range(16):
                    def fq(h, dt=dt):
                        b = (dt % 2) * 512
                        return [h.dma_start(out=wvstg[:, b:b + 512],
                                            in_=c.wkvv_h[dt * 128:(dt + 1) * 128, :])]
                    qwv[dt] = g.add("Q", fq, deps=[vwv[dt - 2] if dt >= 2 else None])

                    def fv(h, dt=dt):
                        b = (dt % 2) * 512
                        h.tensor_copy(wvbf[:, b:b + 512], wvstg[:, b:b + 512])
                    vwv[dt] = g.add("V", fv, deps=[qwv[dt],
                                                   tv.get((dt - 2, 3))])

                    def ftv2(h, dt=dt):
                        b = (dt % 2) * 512
                        for jt in range(4):
                            h.matmul(ptrB[:, jt * 512:(jt + 1) * 512],
                                     xsel[:, dt * 512 + jt * 128: dt * 512 + (jt + 1) * 128],
                                     wvbf[:, b:b + 512],
                                     start=(dt == 0), stop=(dt == 15))
                    dep0 = [vwv[dt], vxs[dt]]
                    if dt == 0:
                        dep0 += [vxs[15], vgv[7]]
                    tv[(dt, 3)] = g.add("T", ftv2, deps=dep0)

                vksb = [None] * 4
                tksw = [None] * 4
                vkr = [None] * 4
                for ct in range(4):
                    def fvk(h, ct=ct):
                        h.tensor_copy(ksb[:], ptrA[:, ct * 512:(ct + 1) * 512])
                    vksb[ct] = g.add("V", fvk, deps=[tkT[ct],
                                                     tksw[ct - 1] if ct else None,
                                                     vkr[ct - 1] if ct else None])

                    def ftw(h, ct=ct):
                        sl = (ct % 2) * 512
                        h.matmul(ptrA[:, sl:sl + 512], permf[:], ksb[:],
                                 start=True, stop=True)
                    tksw[ct] = g.add("T", ftw, deps=[vksb[ct],
                                                     vkr[ct - 2] if ct >= 2 else None,
                                                     tga[15]])

                    def fkr(h, ct=ct):
                        sl = (ct % 2) * 512
                        h.tensor_tensor(bufk[:, 0:512], ksb[:], CSk[:], ALU.mult)
                        h.tensor_tensor(bufk[:, 512:1024], ptrA[:, sl:sl + 512],
                                        SSk[:], ALU.mult)
                        h.drain()
                        h.tensor_tensor(krope[:, ct * 512:(ct + 1) * 512],
                                        bufk[:, 0:512], bufk[:, 512:1024], ALU.add)
                    vkr[ct] = g.add("V", fkr, deps=[tksw[ct]])

                vvg = [None] * 4
                for jt in range(4):
                    def fvg(h, jt=jt):
                        h.tensor_scalar(vg[:, jt * 512:(jt + 1) * 512],
                                        ptrB[:, jt * 512:(jt + 1) * 512],
                                        gv[:, jt:jt + 1], None, ALU.mult)
                    vvg[jt] = g.add("V", fvg, deps=[tv[(15, 3)], vgv[jt]])

                vcap = [None] * 4
                for jt in range(4):
                    def fcap(h, jt=jt):
                        h.tensor_scalar(capf[:], iotaL[:], gv[:, 4 + jt:5 + jt], None,
                                        ALU.is_lt)
                        h.drain()
                        h.tensor_scalar(cap[:, jt * 2048:(jt + 1) * 2048], capf[:],
                                        NEGC, 1e9, ALU.mult, ALU.add)
                    vcap[jt] = g.add("V", fcap, deps=[s_iol, vgv[4 + jt],
                                                      vcap[jt - 1] if jt else None])
                g.emit()
            barrier()

            with ExitStack() as p3_:
                p3 = p3_.enter_context
                csq = p3(nc.sbuf_tensor("csq3", [128, 2048], F32))
                ssq = p3(nc.sbuf_tensor("ssq3", [128, 2048], F32))
                eT = p3(nc.sbuf_tensor("eT3", [128, 4 * 2048], BF16))
                sm = p3(nc.sbuf_tensor("sm3", [128, 2048], F32))
                qsb = p3(nc.sbuf_tensor("qsb3", [128, 2048], F32))
                qrope = p3(nc.sbuf_tensor("qrope3", [128, 2048], BF16))
                recB = p3(nc.sbuf_tensor("recB3", [128, 2048], F32))
                rec_row = p3(nc.sbuf_tensor("rec3", [1, 2048], F32))
                outb = p3(nc.sbuf_tensor("outb3", [128, 2 * 2048], BF16))
                g = G_(block)

                s_cs = g.add("Q", lambda h: [
                    h.dma_start(out=csq[:], in_=c.csq_h[:]),
                    h.dma_start(out=ssq[:], in_=c.ssq_h[:]),
                ], n_dma=2)

                qwq = [None] * 16
                vwq = [None] * 16
                tpq = [None] * 16
                vqs = [None] * 16
                tqw = [None] * 16
                vqr = [None] * 16
                tsc = {}
                vst = {}
                sex = {}
                tpd = [None] * 16
                vrc = [None] * 16
                trb = [None] * 16
                vrb = [None] * 16
                tpo = [None] * 16
                vout = [None] * 16
                gout = [None] * 16
                for hh in range(16):
                    kvh = hh // 4

                    def fq(h, hh=hh):
                        return [h.dma_start(out=wstg[:], in_=c.wq_h[hh])]
                    qwq[hh] = g.add("Q", fq, deps=[vwq[hh - 1] if hh else None])

                    def fv(h):
                        h.tensor_copy(wbf[:], wstg[:])
                    vwq[hh] = g.add("V", fv, deps=[qwq[hh],
                                                   tpq[hh - 1] if hh else None])

                    def fpq(h):
                        for dt in range(16):
                            for nck in range(4):
                                h.matmul(ptrA[:, nck * 512:(nck + 1) * 512],
                                         wbf[:, dt * 128:(dt + 1) * 128],
                                         xnT[:, dt * 2048 + nck * 512:
                                             dt * 2048 + (nck + 1) * 512],
                                         start=(dt == 0), stop=(dt == 15))
                    tpq[hh] = g.add("T", fpq, deps=[vwq[hh],
                                                    vrb[hh - 1] if hh else None])

                    def fqs(h):
                        h.tensor_copy(qsb[:], ptrA[:])
                    vqs[hh] = g.add("V", fqs, deps=[tpq[hh],
                                                    tqw[hh - 1] if hh else None])

                    def fqw(h):
                        for nck in range(4):
                            h.matmul(ptrB[:, nck * 512:(nck + 1) * 512], permf[:],
                                     qsb[:, nck * 512:(nck + 1) * 512],
                                     start=True, stop=True)
                    tqw[hh] = g.add("T", fqw, deps=[vqs[hh],
                                                    vout[hh - 1] if hh else None])

                    def fqr(h):
                        h.tensor_tensor(qsb[:], qsb[:], csq[:], ALU.mult)
                        h.tensor_tensor(sm[:], ptrB[:], ssq[:], ALU.mult)
                        h.drain()
                        h.tensor_tensor(qrope[:], qsb[:], sm[:], ALU.add)
                    vqr[hh] = g.add("V", fqr, deps=[tqw[hh], s_cs])

                    for jt in range(4):
                        def fsc(h, hh=hh, jt=jt, kvh=kvh):
                            for nck in range(4):
                                h.matmul(ptrA[:, nck * 512:(nck + 1) * 512],
                                         krope[:, kvh * 512 + jt * 128:
                                               kvh * 512 + (jt + 1) * 128],
                                         qrope[:, nck * 512:(nck + 1) * 512],
                                         start=True, stop=True)
                        dsc = [vqr[hh]]
                        dsc.append(vqs[hh] if jt == 0 else vst[(hh, jt - 1)])
                        tsc[(hh, jt)] = g.add("T", fsc, deps=dsc)

                        def fstt(h, jt=jt):
                            h.scalar_tensor_tensor(
                                sm[:], ptrA[:], SCL,
                                cap[:, jt * 2048:(jt + 1) * 2048],
                                ALU.mult, ALU.min)
                        dst_ = [tsc[(hh, jt)], vqr[hh]]
                        if (hh, jt - 1) in sex:
                            dst_.append(sex[(hh, jt - 1)])
                        vst[(hh, jt)] = g.add("V", fstt, deps=dst_)

                        def fexp(h, jt=jt):
                            h.activation(eT[:, jt * 2048:(jt + 1) * 2048], sm[:],
                                         AF.Exp)
                        dex = [vst[(hh, jt)]]
                        if hh >= 1:
                            dex.append(tpo[hh - 1])
                            dex.append(tpd[hh - 1])
                        sex[(hh, jt)] = g.add("S", fexp, deps=dex)

                    def fpd(h):
                        for jt in range(4):
                            for nck in range(4):
                                h.matmul(ptrA[0:1, nck * 512:(nck + 1) * 512],
                                         onec[:, 0:1],
                                         eT[:, jt * 2048 + nck * 512:
                                            jt * 2048 + (nck + 1) * 512],
                                         start=(jt == 0), stop=(jt == 3))
                    tpd[hh] = g.add("T", fpd, deps=[sex[(hh, 3)], vst[(hh, 3)]])

                    def frc(h):
                        h.activation(rec_row[:], ptrA[0:1, :], AF.Ln)
                        h.drain()
                        h.activation(rec_row[:], rec_row[:], AF.Exp, scale=-1.0)
                    vrc[hh] = g.add("S", frc, deps=[tpd[hh]])

                    def frb(h):
                        for nck in range(4):
                            h.matmul(ptrA[:, nck * 512:(nck + 1) * 512], ones1[:],
                                     rec_row[0:1, nck * 512:(nck + 1) * 512],
                                     start=True, stop=True)
                    trb[hh] = g.add("T", frb, deps=[vrc[hh]])

                    def fvrb(h):
                        h.tensor_copy(recB[:], ptrA[:])
                    vrb[hh] = g.add("V", fvrb, deps=[trb[hh]])

                    def fpo(h, kvh=kvh):
                        for jt in range(4):
                            for nck in range(4):
                                h.matmul(ptrB[:, nck * 512:(nck + 1) * 512],
                                         vg[:, jt * 512 + kvh * 128:
                                            jt * 512 + (kvh + 1) * 128],
                                         eT[:, jt * 2048 + nck * 512:
                                            jt * 2048 + (nck + 1) * 512],
                                         start=(jt == 0), stop=(jt == 3))
                    tpo[hh] = g.add("T", fpo, deps=[sex[(hh, 3)], vqr[hh]])

                    def fout(h, hh=hh):
                        b = (hh % 2) * 2048
                        h.tensor_tensor(outb[:, b:b + 2048], ptrB[:], recB[:],
                                        ALU.mult)
                    vout[hh] = g.add("V", fout, deps=[tpo[hh], vrb[hh],
                                                      gout[hh - 2] if hh >= 2 else None])

                    def fgo(h, hh=hh):
                        b = (hh % 2) * 2048
                        return [h.dma_start(out=outdr[hh], in_=outb[:, b:b + 2048])]
                    gout[hh] = g.add("G", fgo, deps=[vout[hh]])
                g.emit()
                p3_marks = {"gout": [g.steps[gout[hh]][3] for hh in range(16)]}
            barrier()

        with ExitStack() as p4_:
            p4 = p4_.enter_context
            wo_bf = p4(nc.sbuf_tensor("wobf4", [128, ND * 2048], BF16))
            otg = p4(nc.sbuf_tensor("otg4", [128, 2 * 2048], BF16))
            rstg2 = p4(nc.sbuf_tensor("rstg24", [128, 2 * 2048], F32))
            bufA = p4(nc.sbuf_tensor("bufA4", [128, 2048], F32))
            bufB = p4(nc.sbuf_tensor("bufB4", [128, 2048], F32))
            trash = p4(nc.sbuf_tensor("trash4", [128, 2048], BF16))
            g = G_(block)
            qwo = [None] * 16
            vwo = [None] * 16
            for et in range(16):
                def fq(h, et=et):
                    return [h.dma_start(out=wstg[:],
                                        in_=c.wo_h[et * 128:(et + 1) * 128, :])]
                qwo[et] = g.add("Q", fq, deps=[vwo[et - 1] if et else None])

                def fv(h, et=et):
                    h.tensor_copy(wo_bf[:, et * 2048:(et + 1) * 2048], wstg[:])
                vwo[et] = g.add("V", fv, deps=[qwo[et]])

            qot = [None] * NT
            qrb = [None] * NT
            ttp = [None] * NT
            vpj = [None] * NT
            vrd = [None] * NT
            ssq4 = [None] * NT
            vvar = [None] * NT
            srst = [None] * NT
            vres = [None] * NT
            gst = [None] * NT
            for tt in range(NT):
                def fqo(h, tt=tt):
                    b = (tt % 2) * 2048
                    return [h.dma_start(out=otg[:, b + et * 128: b + (et + 1) * 128],
                                        in_=outdr[et][:, tt * 128:(tt + 1) * 128])
                            for et in range(16)]
                qot[tt] = g.add("Q", fqo, n_dma=16,
                                deps=[ttp[tt - 2] if tt >= 2 else None],
                                predeps=[p3_marks["gout"][et] for et in range(16)]
                                if tt == 0 else ())

                def fqr2(h, tt=tt):
                    b = (tt % 2) * 2048
                    return [h.dma_start(out=rstg2[:, b:b + 2048],
                                        in_=rB[tt * 128:(tt + 1) * 128, :])]
                qrb[tt] = g.add("G", fqr2, deps=[vres[tt - 2] if tt >= 2 else None,
                                                 gst[tt - 2] if tt >= 2 else None])

                def ftp(h, tt=tt):
                    b = (tt % 2) * 2048
                    for et in range(16):
                        for nck in range(4):
                            h.matmul(ptrA[:, nck * 512:(nck + 1) * 512],
                                     otg[:, b + et * 128: b + (et + 1) * 128],
                                     wo_bf[:, et * 2048 + nck * 512:
                                           et * 2048 + (nck + 1) * 512],
                                     start=(et == 0), stop=(et == 15))
                ttp[tt] = g.add("T", ftp, deps=[qot[tt], vwo[15],
                                                vpj[tt - 1] if tt else None])

                def fpj(h):
                    h.tensor_copy(bufA[:], ptrA[:])
                vpj[tt] = g.add("V", fpj, deps=[ttp[tt],
                                                vres[tt - 1] if tt else None])

                def frd(h, tt=tt):
                    h.reduce_sum(cst[:, tt:tt + 1], bufA[:], axis=AX)
                vrd[tt] = g.add("V", frd, deps=[vpj[tt]])

                def fsq(h, tt=tt):
                    h.activation(trash[:], bufA[:], AF.Square,
                                 accum_out=cst[:, 32 + tt:32 + tt + 1])
                ssq4[tt] = g.add("S", fsq, deps=[vpj[tt],
                                                 ssq4[tt - 1] if tt else None])

                def fvar(h, tt=tt):
                    h.tensor_scalar_mul(cst[:, 64 + tt:64 + tt + 1], cst[:, tt:tt + 1],
                                        1.0 / D)
                    h.tensor_scalar_mul(cst[:, 32 + tt:32 + tt + 1],
                                        cst[:, 32 + tt:32 + tt + 1], 1.0 / D)
                    h.drain()
                    h.tensor_tensor(cst[:, 96 + tt:96 + tt + 1],
                                    cst[:, 64 + tt:64 + tt + 1],
                                    cst[:, 64 + tt:64 + tt + 1], ALU.mult)
                    h.drain()
                    h.tensor_sub(cst[:, 32 + tt:32 + tt + 1],
                                 cst[:, 32 + tt:32 + tt + 1],
                                 cst[:, 96 + tt:96 + tt + 1])
                    h.drain()
                    h.tensor_scalar_add(cst[:, 32 + tt:32 + tt + 1],
                                        cst[:, 32 + tt:32 + tt + 1], EPS)
                vvar[tt] = g.add("V", fvar, deps=[vrd[tt], ssq4[tt]])

                def fsr(h, tt=tt):
                    h.activation(cst[:, 32 + tt:32 + tt + 1],
                                 cst[:, 32 + tt:32 + tt + 1], AF.Ln)
                    h.drain()
                    h.activation(cst[:, 96 + tt:96 + tt + 1],
                                 cst[:, 32 + tt:32 + tt + 1], AF.Exp, scale=-0.5)
                srst[tt] = g.add("S", fsr, deps=[vvar[tt]])

                def fres(h, tt=tt):
                    b = (tt % 2) * 2048
                    h.tensor_scalar(bufB[:], bufA[:], cst[:, 64 + tt:64 + tt + 1],
                                    cst[:, 96 + tt:96 + tt + 1],
                                    ALU.subtract, ALU.mult)
                    h.drain()
                    h.tensor_add(rstg2[:, b:b + 2048], rstg2[:, b:b + 2048], bufB[:])
                vres[tt] = g.add("V", fres, deps=[srst[tt], qrb[tt]])

                def fst2(h, tt=tt):
                    b = (tt % 2) * 2048
                    return [h.dma_start(out=out_dst3[tt * 128:(tt + 1) * 128, :],
                                        in_=rstg2[:, b:b + 2048])]
                gst[tt] = g.add("G", fst2, deps=[vres[tt]])
            g.emit()
            c.p4_store_marks = [g.steps[gst[tt]][3] for tt in range(NT)]
        barrier()
    return c.p4_store_marks


def emit_phase4(c):
    nc, block, barrier = c.nc, c.block, c.barrier
    sT, sV, sS, sP, sO = c.sT, c.sV, c.sS, c.sP, c.sO
    idf, idb, rA, y_h = c.idf, c.idb, c.rA, c.y_h
    NO = 43
    CHUNKS = [(s, min(4, NO - s)) for s in range(0, NO, 4)]

    def G_(block):
        return Graph(block, sT, sV, sS, sP, [c.dQa, c.dQb], [c.dGa, c.dGb])

    with ExitStack() as po_:
        oe = po_.enter_context
        cst = oe(nc.sbuf_tensor("cst4", [128, 128], F32))
        mcol = oe(nc.sbuf_tensor("mcol4", [128, NT], F32))
        rcol = oe(nc.sbuf_tensor("rcol4", [128, NT], F32))

        with ExitStack() as px_:
            xe = px_.enter_context
            xnT = xe(nc.sbuf_tensor("xnT4", [128, ND * 2048], BF16))
            with ExitStack() as p1_:
                p1 = p1_.enter_context
                rstg = p1(nc.sbuf_tensor("rstg41", [128, 2 * 2048], F32))
                xnbuf = p1(nc.sbuf_tensor("xnbuf4", [128, 2 * 2048], BF16))
                trash = p1(nc.sbuf_tensor("trash41", [128, 2048], BF16))
                ptrA = p1(nc.psum_tensor("pA4", [128, 2048], F32))
                ptrB = p1(nc.psum_tensor("pB4", [128, 2048], F32))

                g = G_(block)
                c.bg_alloc(p1)
                ld = [None] * NT
                sq = [None] * NT
                rs = [None] * NT
                for tt in range(NT):
                    def fl(h, tt=tt):
                        b = (tt % 2) * 2048
                        return [h.dma_start(out=rstg[:, b:b + 2048],
                                            in_=rA[tt * 128:(tt + 1) * 128, :])]
                    ld[tt] = g.add("Q", fl, deps=[rs[tt - 2] if tt >= 2 else None,
                                                  sq[tt - 2] if tt >= 2 else None],
                                   predeps=(list(c.p3_marks) + [(sO, sO.n)])
                                   if tt == 0 else ())

                    def fs(h, tt=tt):
                        b = (tt % 2) * 2048
                        h.activation(trash[:], rstg[:, b:b + 2048], AF.Square,
                                     accum_out=cst[:, 32 + tt:32 + tt + 1])
                    sq[tt] = g.add("S", fs, deps=[ld[tt], sq[tt - 1] if tt else None])

                    def fv(h, tt=tt):
                        b = (tt % 2) * 2048
                        h.reduce_sum(cst[:, tt:tt + 1], rstg[:, b:b + 2048], axis=AX)
                    rs[tt] = g.add("V", fv, deps=[ld[tt]])

                def fstat(h):
                    h.tensor_scalar_mul(mcol[:, 0:NT], cst[:, 0:NT], 1.0 / D)
                    h.tensor_scalar_mul(cst[:, 32:32 + NT], cst[:, 32:32 + NT], 1.0 / D)
                    h.drain()
                    h.tensor_tensor(cst[:, 64:64 + NT], mcol[:, 0:NT], mcol[:, 0:NT],
                                    ALU.mult)
                    h.drain()
                    h.tensor_sub(cst[:, 32:32 + NT], cst[:, 32:32 + NT],
                                 cst[:, 64:64 + NT])
                    h.drain()
                    h.tensor_scalar_add(cst[:, 32:32 + NT], cst[:, 32:32 + NT], EPS)
                s_stat = g.add("V", fstat, deps=rs + sq)

                def frstd(h):
                    h.activation(cst[:, 32:32 + NT], cst[:, 32:32 + NT], AF.Ln)
                    h.drain()
                    h.activation(rcol[:, 0:NT], cst[:, 32:32 + NT], AF.Exp, scale=-0.5)
                s_rstd = g.add("S", frstd, deps=[s_stat])

                ld2 = [None] * NT
                cast = [None] * NT
                trg = {}
                tev = {}
                for tt in range(NT):
                    def fl2(h, tt=tt):
                        b = (tt % 2) * 2048
                        return [h.dma_start(out=rstg[:, b:b + 2048],
                                            in_=rA[tt * 128:(tt + 1) * 128, :])]
                    dp = [rs[NT - 2 + tt], sq[NT - 2 + tt]] if tt < 2 else [cast[tt - 2]]
                    ld2[tt] = g.add("Q", fl2, deps=dp)

                    def fc(h, tt=tt):
                        b = (tt % 2) * 2048
                        h.tensor_scalar(xnbuf[:, b:b + 2048], rstg[:, b:b + 2048],
                                        mcol[:, tt:tt + 1], rcol[:, tt:tt + 1],
                                        ALU.subtract, ALU.mult)
                    cast[tt] = g.add("V", fc, deps=[ld2[tt], s_rstd,
                                                    trg.get((tt - 2, 3))])
                    for gi in range(4):
                        def ft(h, tt=tt, gi=gi):
                            ps = ptrA if gi % 2 == 0 else ptrB
                            for j in range(4):
                                dt = gi * 4 + j
                                b = (tt % 2) * 2048
                                h.matmul(ps[:, j * 128:(j + 1) * 128],
                                         xnbuf[:, b + dt * 128: b + (dt + 1) * 128],
                                         idb[:], start=True, stop=True)
                        k = tt * 4 + gi
                        pk = (divmod(k - 2, 4)) if k >= 2 else None
                        trg[(tt, gi)] = g.add("T", ft, deps=[cast[tt],
                                                             tev.get(pk) if pk else None])

                        def fe(h, tt=tt, gi=gi):
                            ps = ptrA if gi % 2 == 0 else ptrB
                            for j in range(4):
                                dt = gi * 4 + j
                                h.tensor_copy(
                                    xnT[:, dt * 2048 + tt * 128: dt * 2048 + (tt + 1) * 128],
                                    ps[:, j * 128:(j + 1) * 128])
                        tev[(tt, gi)] = g.add("V", fe, deps=[trg[(tt, gi)]])
                    c.bg_pump(g, 1)
                c.bg_pump(g, 12)
                g.emit()

                def _cc(p):
                    for mc, cnt in c.jg_marks:
                        p.wait_ge(mc.sem, cnt)
                    sP.inc(p.collective_compute(
                        "AllGather",
                        mybir.AluOpType.bypass,
                        ins=[c.agsrc[:, :, :]],
                        outs=[c.agdst[:, :, :, :]],
                        replica_groups=[list(range(8))],
                    ))
                    p.wait_ge(sP.sem, sP.n)
                block.gpsimd(_cc)
                cc_mark = (sP, sP.n)
            barrier()

            with ExitStack() as pa_:
                pa = pa_.enter_context
                wgc = pa(nc.sbuf_tensor("wgc4", [128, 2 * 2048], BF16))
                wuc = pa(nc.sbuf_tensor("wuc4", [128, 2 * 2048], BF16))
                sgb = pa(nc.sbuf_tensor("sgb4", [128, 2048], BF16))
                gub = pa(nc.sbuf_tensor("gub4", [128, 2048], BF16))
                hstg = pa(nc.sbuf_tensor("hstg4", [128, 2 * 2048], BF16))
                pG = pa(nc.psum_tensor("pG4", [128, 2048], F32))
                pU = pa(nc.psum_tensor("pU4", [128, 2048], F32))
                g = G_(block)
                wsg = {}
                wsu = {}
                tg = {}
                tu = {}
                ssig = {}
                vgu = {}
                vh = {}
                gst = {}
                for os in range(NO):
                    b0 = (os % 2) * 2048

                    def fqg(h, os=os, b0=b0):
                        return [h.dma_start(out=wgc[:, b0 + ht * 128: b0 + (ht + 1) * 128],
                                            in_=c.agdst[ht // 2, ht % 2][:, os * 128:(os + 1) * 128])
                                for ht in range(16)]
                    wsg[os] = g.add("Q", fqg, n_dma=16, deps=[tg.get(os - 2)],
                                    predeps=[cc_mark] if os == 0 else ())

                    def fqu(h, os=os, b0=b0):
                        return [h.dma_start(out=wuc[:, b0 + ht * 128: b0 + (ht + 1) * 128],
                                            in_=c.agdst[ht // 2, 2 + ht % 2][:, os * 128:(os + 1) * 128])
                                for ht in range(16)]
                    wsu[os] = g.add("G", fqu, n_dma=16, deps=[tu.get(os - 2)])

                    def ftg(h, os=os):
                        b = (os % 2) * 2048
                        for ht in range(16):
                            for nck in range(4):
                                h.matmul(pG[:, nck * 512:(nck + 1) * 512],
                                         wgc[:, b + ht * 128: b + (ht + 1) * 128],
                                         xnT[:, ht * 2048 + nck * 512:
                                             ht * 2048 + (nck + 1) * 512],
                                         start=(ht == 0), stop=(ht == 15))
                    tg[os] = g.add("T", ftg,
                                   deps=[wsg[os], vgu.get(os - 1),
                                         ssig.get(os - 1)])

                    def ftu(h, os=os):
                        b = (os % 2) * 2048
                        for ht in range(16):
                            for nck in range(4):
                                h.matmul(pU[:, nck * 512:(nck + 1) * 512],
                                         wuc[:, b + ht * 128: b + (ht + 1) * 128],
                                         xnT[:, ht * 2048 + nck * 512:
                                             ht * 2048 + (nck + 1) * 512],
                                         start=(ht == 0), stop=(ht == 15))
                    tu[os] = g.add("T", ftu, deps=[wsu[os], vh.get(os - 1)])

                    def fsig(h):
                        h.activation(sgb[:], pG[:], AF.Sigmoid)
                    ssig[os] = g.add("S", fsig, deps=[tg[os], vgu.get(os - 1)])

                    def fgu(h):
                        h.tensor_tensor(gub[:], pG[:], sgb[:], ALU.mult)
                    vgu[os] = g.add("V", fgu,
                                    deps=[tg[os], ssig[os], vh.get(os - 1)])

                    def fvh(h, os=os):
                        b = (os % 2) * 2048
                        h.tensor_tensor(hstg[:, b:b + 2048], gub[:], pU[:],
                                        ALU.mult)
                    vh[os] = g.add("V", fvh,
                                   deps=[vgu[os], tu[os], gst.get(os - 2)])

                    def fgs(h, os=os):
                        b = (os % 2) * 2048
                        return [h.dma_start(out=c.hTdr[os],
                                            in_=hstg[:, b:b + 2048])]
                    gst[os] = g.add("G", fgs, deps=[vh[os]])

                g.emit()
                hT_marks = [g.steps[gst[k]][3] for k in gst]
            barrier()

        with ExitStack() as pb_:
            pb = pb_.enter_context
            hTh = pb(nc.sbuf_tensor("hTh4", [128, NO * 512], BF16))
            wds0 = pb(nc.sbuf_tensor("wds04", [128, 5504], BF16))
            wds1 = pb(nc.sbuf_tensor("wds14", [128, 5504], BF16))
            dtb = pb(nc.sbuf_tensor("dtb4", [128, 2 * 512], BF16))
            outb = pb(nc.sbuf_tensor("outb4", [128, 4 * 2048], BF16))
            rstg = pb(nc.sbuf_tensor("rstg43", [128, 2 * 2048], F32))
            bufA = pb(nc.sbuf_tensor("bufA43", [128, 2048], F32))
            bufB = pb(nc.sbuf_tensor("bufB43", [128, 2048], F32))
            trash = pb(nc.sbuf_tensor("trash43", [128, 2048], BF16))
            pd0 = pb(nc.psum_tensor("pd04", [128, 512], F32))
            pd1 = pb(nc.psum_tensor("pd14", [128, 512], F32))
            px0 = pb(nc.psum_tensor("px04", [128, 512], F32))
            px1 = pb(nc.psum_tensor("px14", [128, 512], F32))
            g = G_(block)
            qh = {}
            dec = {}
            gw = {}
            tdn = {}
            vev = {}
            ttx = {}
            vcp = {}
            vpj = {}
            vres = {}
            gsty = {}
            qrl = {}
            ssqd = {}
            for q in range(4):
                def fqh(h, q=q):
                    return [h.dma_start(out=hTh[:, os * 512:(os + 1) * 512],
                                        in_=c.hTdr[os][:, q * 512:(q + 1) * 512])
                            for os in range(NO)]
                qh[q] = g.add("Q", fqh, n_dma=NO,
                              deps=[tdn.get((q - 1, 15))],
                              predeps=hT_marks if q == 0 else ())
                for ds in range(16):
                    wds = wds0 if ds % 2 == 0 else wds1

                    def fqw(h, ds=ds, wds=wds):
                        return [h.dma_start(out=wds[:],
                                            in_=c.agdst[ds // 2, 4 + ds % 2])]
                    wsrc = g.add("Q", fqw, deps=[tdn.get((q, ds - 2))])

                    def ftd(h, ds=ds, wds=wds):
                        ps = pd0 if ds % 2 == 0 else pd1
                        for os in range(NO):
                            h.matmul(ps[:], wds[:, os * 128:(os + 1) * 128],
                                     hTh[:, os * 512:(os + 1) * 512],
                                     start=(os == 0), stop=(os == NO - 1))
                    tdn[(q, ds)] = g.add("T", ftd,
                                         deps=[wsrc, qh[q], vev.get((q, ds - 2)),
                                               vev.get((q - 1, 14 + ds)) if ds < 2 else None])

                    def fev(h, ds=ds):
                        ps = pd0 if ds % 2 == 0 else pd1
                        b = (ds % 2) * 512
                        h.tensor_copy(dtb[:, b:b + 512], ps[:])
                    vev[(q, ds)] = g.add("V", fev,
                                         deps=[tdn[(q, ds)], ttx.get((q, ds - 2)),
                                               ttx.get((q - 1, 14 + ds)) if ds < 2 else None])

                    def ftx(h, ds=ds):
                        px = px0 if ds % 2 == 0 else px1
                        b = (ds % 2) * 512
                        for tb in range(4):
                            h.matmul(px[:, tb * 128:(tb + 1) * 128],
                                     dtb[:, b + tb * 128: b + (tb + 1) * 128],
                                     idb[:], start=True, stop=True)
                    ttx[(q, ds)] = g.add("T", ftx,
                                         deps=[vev[(q, ds)], vcp.get((q, ds - 2)),
                                               vcp.get((q - 1, 14 + ds)) if ds < 2 else None])

                    def fcp(h, ds=ds):
                        px = px0 if ds % 2 == 0 else px1
                        for tb in range(4):
                            h.tensor_copy(outb[:, tb * 2048 + ds * 128:
                                               tb * 2048 + (ds + 1) * 128],
                                          px[:, tb * 128:(tb + 1) * 128])
                    vcp[(q, ds)] = g.add("V", fcp, deps=[ttx[(q, ds)]])

                for tb in range(4):
                    tt = q * 4 + tb

                    def fqr(h, tt=tt):
                        b = (tt % 2) * 2048
                        return [h.dma_start(out=rstg[:, b:b + 2048],
                                            in_=rA[tt * 128:(tt + 1) * 128, :])]
                    qrl[tt] = g.add("G", fqr,
                                    deps=[vres.get(tt - 2), gsty.get(tt - 2)])

                    def fpj(h, tb=tb):
                        h.tensor_copy(bufA[:], outb[:, tb * 2048:(tb + 1) * 2048])
                    vpj[tt] = g.add("V", fpj,
                                    deps=[vcp[(q, 15)], vres.get(tt - 1),
                                          ssqd.get(tt - 1)])

                    def frd(h, tt=tt):
                        h.reduce_sum(cst[:, tt:tt + 1], bufA[:], axis=AX)
                    vrd = g.add("V", frd, deps=[vpj[tt]])

                    def fsq(h, tt=tt):
                        h.activation(trash[:], bufA[:], AF.Square,
                                     accum_out=cst[:, 32 + tt:32 + tt + 1])
                    ssq = ssqd[tt] = g.add("S", fsq, deps=[vpj[tt]])

                    def fvar(h, tt=tt):
                        h.tensor_scalar_mul(cst[:, 64 + tt:64 + tt + 1],
                                            cst[:, tt:tt + 1], 1.0 / D)
                        h.tensor_scalar_mul(cst[:, 32 + tt:32 + tt + 1],
                                            cst[:, 32 + tt:32 + tt + 1], 1.0 / D)
                        h.drain()
                        h.tensor_tensor(cst[:, 96 + tt:96 + tt + 1],
                                        cst[:, 64 + tt:64 + tt + 1],
                                        cst[:, 64 + tt:64 + tt + 1], ALU.mult)
                        h.drain()
                        h.tensor_sub(cst[:, 32 + tt:32 + tt + 1],
                                     cst[:, 32 + tt:32 + tt + 1],
                                     cst[:, 96 + tt:96 + tt + 1])
                        h.drain()
                        h.tensor_scalar_add(cst[:, 32 + tt:32 + tt + 1],
                                            cst[:, 32 + tt:32 + tt + 1], EPS)
                    vvar = g.add("V", fvar, deps=[vrd, ssq])

                    def fsr(h, tt=tt):
                        h.activation(cst[:, 32 + tt:32 + tt + 1],
                                     cst[:, 32 + tt:32 + tt + 1], AF.Ln)
                        h.drain()
                        h.activation(cst[:, 96 + tt:96 + tt + 1],
                                     cst[:, 32 + tt:32 + tt + 1], AF.Exp, scale=-0.5)
                    srst = g.add("S", fsr, deps=[vvar])

                    def fres(h, tt=tt):
                        b = (tt % 2) * 2048
                        h.tensor_scalar(bufB[:], bufA[:], cst[:, 64 + tt:64 + tt + 1],
                                        cst[:, 96 + tt:96 + tt + 1],
                                        ALU.subtract, ALU.mult)
                        h.drain()
                        h.tensor_add(rstg[:, b:b + 2048], rstg[:, b:b + 2048],
                                     bufB[:])
                    vres[tt] = g.add("V", fres, deps=[srst, qrl[tt]])

                    def fsty(h, tt=tt):
                        b = (tt % 2) * 2048
                        return [h.dma_start(out=y_h[tt * 128:(tt + 1) * 128, :],
                                            in_=rstg[:, b:b + 2048])]
                    gsty[tt] = g.add("G", fsty, deps=[vres[tt]])
            g.emit()
            y_marks = [g.steps[gsty[tt]][3] for tt in range(NT)]
        barrier()
    return y_marks


def build(n_phases=N_PHASES_ALL, stop_after=None, n_et=16, gate_cut=99):
    nc = bass.Bass("TRN2", target_bir_lowering=False)

    x_h = nc.dram_tensor("x", [T_SEQ, D], F32, kind="ExternalInput").ap()
    y_h = nc.dram_tensor("y", [T_SEQ, D], F32, kind="ExternalOutput").ap()
    wgT_h = nc.dram_tensor("wgT", [16, 128, 2048], F32, kind="ExternalInput").ap()
    wiT_h = nc.dram_tensor("wiT", [16, 128, 2048], F32, kind="ExternalInput").ap()
    bin_h = nc.dram_tensor("binc", [128, 16], F32, kind="ExternalInput").ap()
    bgt_h = nc.dram_tensor("bgtc", [128, 16], F32, kind="ExternalInput").ap()
    lam_h = nc.dram_tensor("lamc", [128, 16], F32, kind="ExternalInput").ap()
    idf_h = nc.dram_tensor("idf", [128, 128], F32, kind="ExternalInput").ap()
    idb_h = nc.dram_tensor("idb", [128, 128], BF16, kind="ExternalInput").ap()

    rA = nc.dram_tensor("rA", [T_SEQ, D], F32, kind="Internal").ap()
    rB = nc.dram_tensor("rB", [T_SEQ, D], F32, kind="Internal").ap()
    h_dr = nc.dram_tensor("h_dr", [T_SEQ, D], BF16, kind="Internal").ap()
    skd = nc.dram_tensor("skd", [1, D], F32, kind="Internal").ap()
    fker_h = nc.dram_tensor("fker", [D, D], F32, kind="ExternalInput").ap()
    fbias_h = nc.dram_tensor("fbias", [1, D], F32, kind="ExternalInput").ap()
    wq_h = nc.dram_tensor("wqt", [16, 128, 2048], F32, kind="ExternalInput").ap()
    wkvk_h = nc.dram_tensor("wkvk", [4, 128, 2048], F32, kind="ExternalInput").ap()
    wkvv_h = nc.dram_tensor("wkvv", [D, 512], F32, kind="ExternalInput").ap()
    wo_h = nc.dram_tensor("wo", [D, D], F32, kind="ExternalInput").ap()
    w1_h = nc.dram_tensor("w1t", [4, 128, 2048], F32, kind="ExternalInput").ap()
    b1c_h = nc.dram_tensor("b1c", [128, 4], F32, kind="ExternalInput").ap()
    w2c_h = nc.dram_tensor("w2c", [128, 4], F32, kind="ExternalInput").ap()
    b2s_h = nc.dram_tensor("b2s", [1, 1], F32, kind="ExternalInput").ap()
    csq_h = nc.dram_tensor("csq", [128, 2048], F32, kind="ExternalInput").ap()
    ssq_h = nc.dram_tensor("ssq", [128, 2048], F32, kind="ExternalInput").ap()
    ktcs_h = nc.dram_tensor("ktcs", [16, 128, 128], F32, kind="ExternalInput").ap()
    ktss_h = nc.dram_tensor("ktss", [16, 128, 128], F32, kind="ExternalInput").ap()
    iotc_h = nc.dram_tensor("iotc", [128, 16], F32, kind="ExternalInput").ap()
    perm_h = nc.dram_tensor("perm", [128, 128], F32, kind="ExternalInput").ap()
    outdr = nc.dram_tensor("outdr", [16, 128, 2048], BF16, kind="Internal").ap()
    xndr = nc.dram_tensor("xndr", [16, 16, 128, 128], BF16, kind="Internal").ap()
    lutb_h = nc.dram_tensor("lutB", [128, 16], F32, kind="ExternalInput").ap()
    myidx_h = nc.dram_tensor("myIdx", [6, 128, 5504], I32, kind="ExternalInput").ap()
    hTdr = nc.dram_tensor("hTdr", [43, 128, 2048], BF16, kind="Internal").ap()
    agsrc = nc.dram_tensor("agsrc", [6, 128, 5504], BF16, kind="Internal").ap()
    agdst = nc.dram_tensor("agdst", [8, 6, 128, 5504], BF16, kind="Internal",
                           addr_space="Shared").ap()

    with ExitStack() as gctx:
        ge = gctx.enter_context
        sT = Ctr(ge(nc.semaphore("sT")))
        sV = Ctr(ge(nc.semaphore("sV")))
        sS = Ctr(ge(nc.semaphore("sS")))
        bar = Ctr(ge(nc.semaphore("bar")))
        sO = Ctr(ge(nc.semaphore("sO")))
        dC = Ctr(ge(nc.semaphore("dC")))
        dA0 = Ctr(ge(nc.semaphore("dA0")))
        dA1 = Ctr(ge(nc.semaphore("dA1")))
        dB0 = Ctr(ge(nc.semaphore("dB0")))
        dB1 = Ctr(ge(nc.semaphore("dB1")))
        dWg = Ctr(ge(nc.semaphore("dWg")))
        dWi = Ctr(ge(nc.semaphore("dWi")))
        dXc = Ctr(ge(nc.semaphore("dXc")))
        dA2 = Ctr(ge(nc.semaphore("dA2")))
        dA3 = Ctr(ge(nc.semaphore("dA3")))
        sP = Ctr(ge(nc.semaphore("sP")))
        dQa = Ctr(ge(nc.semaphore("dQa")))
        dQb = Ctr(ge(nc.semaphore("dQb")))
        dGa = Ctr(ge(nc.semaphore("dGa")))
        dGb = Ctr(ge(nc.semaphore("dGb")))
        dH0 = Ctr(ge(nc.semaphore("dH0")))
        dH1 = Ctr(ge(nc.semaphore("dH1")))
        dRA = Ctr(ge(nc.semaphore("dRA")))

        idf = ge(nc.sbuf_tensor("idf_sb", [128, 128], F32))
        idb = ge(nc.sbuf_tensor("idb_sb", [128, 128], BF16))
        binc = ge(nc.sbuf_tensor("binc_sb", [128, 16], F32))
        bgtc = ge(nc.sbuf_tensor("bgtc_sb", [128, 16], F32))
        negsp = ge(nc.sbuf_tensor("negsp_sb", [128, 16], F32))
        lutb_sb = ge(nc.sbuf_tensor("lutb_sb", [128, 16], F32))

        block = ge(nc.Block())
        n_bar = [0]

        def barrier():
            n_bar[0] += 1
            tgt = 5 * n_bar[0]
            for sec in (block.sync, block.tensor, block.vector, block.scalar, block.gpsimd):
                def _f(e, tgt=tgt):
                    e.drain().then_inc(bar.sem, 1)
                    e.wait_ge(bar.sem, tgt)
                sec(_f)
            bar.n = tgt

        with ExitStack() as ph:
            pe = ph.enter_context
            lam = pe(nc.sbuf_tensor("lam_sb", [128, 16], F32))
            t0 = pe(nc.sbuf_tensor("t0_sb", [128, 16], F32))
            t1 = pe(nc.sbuf_tensor("t1_sb", [128, 16], F32))

            def _q(q):
                for dst, srch in ((idf, idf_h), (idb, idb_h), (binc, bin_h),
                                  (bgtc, bgt_h), (lam, lam_h),
                                  (lutb_sb, lutb_h)):
                    q.dma_start(out=dst[:], in_=srch[:]).then_inc(dC.sem, 16)
                    dC.n += 16
            block.sync(_q)
            q_consts = dC.n

            def _s(s):
                s.wait_ge(dC.sem, q_consts)
                s.activation(t0[:], lam[:], AF.Abs)
                s.drain()
                sS.inc(s.activation(t0[:], t0[:], AF.Exp, scale=-1.0))
            block.scalar(_s)
            c1 = sS.n

            def _v(v):
                v.wait_ge(sS.sem, c1)
                sV.inc(v.tensor_scalar_add(t0[:], t0[:], 1.0))
            block.vector(_v)
            c2 = sV.n

            def _s2(s):
                s.wait_ge(sV.sem, c2)
                sS.inc(s.activation(t0[:], t0[:], AF.Ln))
            block.scalar(_s2)
            c3 = sS.n

            def _v2(v):
                v.wait_ge(sS.sem, c3)
                v.tensor_scalar_max(t1[:], lam[:], 0.0)
                v.drain()
                v.tensor_add(t0[:], t0[:], t1[:])
                v.drain()
                sV.inc(v.tensor_scalar_mul(negsp[:], t0[:], -8.0))
            block.vector(_v2)
            barrier()

        out_dst = rA if n_phases > 1 else y_h
        out_ctr = dRA if n_phases > 1 else sO
        with ExitStack() as ph:
            pe = ph.enter_context
            xnT = pe(nc.sbuf_tensor("xnT", [128, ND * 2048], BF16))
            mcol = pe(nc.sbuf_tensor("mcol", [128, NT], F32))
            rcol = pe(nc.sbuf_tensor("rcol", [128, NT], F32))
            sumh = pe(nc.sbuf_tensor("sumh", [128, NT * ND], F32))
            sqh = pe(nc.sbuf_tensor("sqh", [128, NT * ND], F32))
            xst = pe(nc.sbuf_tensor("xst", [128, 2 * 2048], F32))
            xnst = pe(nc.sbuf_tensor("xnst", [128, 2 * 2048], BF16))
            cst = pe(nc.sbuf_tensor("cst", [128, 128], F32))
            trash = pe(nc.sbuf_tensor("trashL0", [128, 2048], BF16))
            wg_bf = pe(nc.sbuf_tensor("wg_bf", [128, 2048], BF16))
            wi_bf = pe(nc.sbuf_tensor("wi_bf", [128, 2048], BF16))
            wst = pe(nc.sbuf_tensor("wstL0", [128, 2 * 2048], F32))
            bufA = pe(nc.sbuf_tensor("bufA", [128, 2048], F32))
            bufB = pe(nc.sbuf_tensor("bufB", [128, 2048], F32))
            bufC = pe(nc.sbuf_tensor("bufC", [128, 2048], F32))
            xcol = pe(nc.sbuf_tensor("xcol", [128, 2048], F32))
            xTe = pe(nc.sbuf_tensor("xTe", [128, 2048], F32))
            hTe = pe(nc.sbuf_tensor("hTe", [128, 2048], F32))
            hstg = pe(nc.sbuf_tensor("hstg", [128, 2 * 512], BF16))
            ptr = pe(nc.psum_tensor("ptrL0", [128, 512], F32))
            ptr2 = pe(nc.psum_tensor("ptr2L0", [128, 512], F32))
            pgg = pe(nc.psum_tensor("pggL0", [128, 2048], F32))
            phh = pe(nc.psum_tensor("phhL0", [128, 512], F32))
            phh2 = pe(nc.psum_tensor("phh2L0", [128, 512], F32))

            t_base = sT.n
            v_base = sV.n
            s_base = sS.n

            s_sq = [s_base + tt + 1 for tt in range(NT)]
            v_stat = [v_base + tt + 1 for tt in range(NT)]
            v_afterA = v_base + NT
            s_rstd = s_base + NT + 1

            def v_xn(tt):
                return v_afterA + tt * 5 + 1

            def v_xevac(tt, g):
                return v_afterA + tt * 5 + 2 + g
            v_afterB = v_afterA + NT * 5

            def t_xn(tt, g):
                return t_base + tt * 4 + g + 1
            t_afterB = t_base + NT * 4

            def _q1(q):
                for tt in range(NT):
                    d = dA0 if tt % 2 == 0 else dA1
                    if tt >= 2:
                        q.wait_ge(sV.sem, v_stat[tt - 2])
                    q.dma_start(out=xst[:, (tt % 2) * 2048:(tt % 2 + 1) * 2048],
                                in_=x_h[tt * 128:(tt + 1) * 128, :]).then_inc(d.sem, 16)
                    d.n += 16
            block.sync(_q1)
            lnx_done = [16 * (tt // 2 + 1) for tt in range(NT)]

            def _s1(s):
                for tt in range(NT):
                    s.wait_ge((dA0 if tt % 2 == 0 else dA1).sem, lnx_done[tt])
                    if tt > 0:
                        s.drain()
                    b = (tt % 2) * 2048
                    sS.inc(s.activation(trash[:], xst[:, b:b + 2048], AF.Square,
                                        accum_out=cst[:, 32 + tt:32 + tt + 1]))
            block.scalar(_s1)

            def _v1(v):
                for tt in range(NT):
                    v.wait_ge((dA0 if tt % 2 == 0 else dA1).sem, lnx_done[tt])
                    v.wait_ge(sS.sem, s_sq[tt])
                    b = (tt % 2) * 2048
                    if tt < NT - 1:
                        sV.inc(v.reduce_sum(cst[:, tt:tt + 1], xst[:, b:b + 2048], axis=AX))
                    else:
                        v.reduce_sum(cst[:, tt:tt + 1], xst[:, b:b + 2048], axis=AX)
                v.drain()
                v.tensor_scalar_mul(mcol[:, 0:NT], cst[:, 0:NT], 1.0 / D)
                v.tensor_scalar_mul(cst[:, 32:32 + NT], cst[:, 32:32 + NT], 1.0 / D)
                v.drain()
                v.tensor_tensor(cst[:, 64:64 + NT], mcol[:, 0:NT], mcol[:, 0:NT], ALU.mult)
                v.drain()
                v.tensor_sub(cst[:, 32:32 + NT], cst[:, 32:32 + NT], cst[:, 64:64 + NT])
                v.drain()
                sV.inc(v.tensor_scalar_add(cst[:, 32:32 + NT], cst[:, 32:32 + NT], EPS))
            block.vector(_v1)
            assert sV.n == v_afterA

            def _s2b(s):
                s.wait_ge(sV.sem, v_afterA)
                s.activation(cst[:, 32:32 + NT], cst[:, 32:32 + NT], AF.Ln)
                s.drain()
                sS.inc(s.activation(rcol[:, 0:NT], cst[:, 32:32 + NT], AF.Exp, scale=-0.5))
            block.scalar(_s2b)
            assert sS.n == s_rstd

            def _g1(g):
                for tt in range(NT):
                    d = dB0 if tt % 2 == 0 else dB1
                    if tt < 2:
                        g.wait_ge(sV.sem, v_stat[NT - 2 + tt])
                    else:
                        g.wait_ge(sV.sem, v_xn(tt - 2))
                    g.dma_start(out=xst[:, (tt % 2) * 2048:(tt % 2 + 1) * 2048],
                                in_=x_h[tt * 128:(tt + 1) * 128, :]).then_inc(d.sem, 16)
                    d.n += 16
            block.gpsimd(_g1)

            def _vB(v):
                v.wait_ge(sS.sem, s_rstd)
                for tt in range(NT):
                    v.wait_ge((dB0 if tt % 2 == 0 else dB1).sem, 16 * (tt // 2 + 1))
                    if tt >= 2:
                        v.wait_ge(sT.sem, t_xn(tt - 2, 3))
                    b = (tt % 2) * 2048
                    sV.inc(v.tensor_scalar(xnst[:, b:b + 2048], xst[:, b:b + 2048],
                                           mcol[:, tt:tt + 1], rcol[:, tt:tt + 1],
                                           ALU.subtract, ALU.mult))
                    for g in range(4):
                        ps = ptr if g % 2 == 0 else ptr2
                        v.wait_ge(sT.sem, t_xn(tt, g))
                        for j in range(4):
                            dt = g * 4 + j
                            v.tensor_copy(
                                xnT[:, dt * 2048 + tt * 128: dt * 2048 + (tt + 1) * 128],
                                ps[:, j * 128:(j + 1) * 128])
                        sV.inc(v.drain())
            block.vector(_vB)
            assert sV.n == v_afterB

            def _tB(t):
                k = 0
                for tt in range(NT):
                    t.wait_ge(sV.sem, v_xn(tt))
                    b = (tt % 2) * 2048
                    for g in range(4):
                        ps = ptr if g % 2 == 0 else ptr2
                        if k >= 2:
                            pt, pg = divmod(k - 2, 4)
                            t.wait_ge(sV.sem, v_xevac(pt, pg))
                        for j in range(4):
                            dt = g * 4 + j
                            t.matmul(ps[:, j * 128:(j + 1) * 128],
                                     xnst[:, b + dt * 128: b + (dt + 1) * 128],
                                     idb[:], start=True, stop=True)
                        sT.inc(t.drain())
                        k += 1
            block.tensor(_tB)
            assert sT.n == t_afterB

            barrier()

            if stop_after == "ln":
                return nc
            NET = n_et
            tb, vb, sb = sT.n, sV.n, sS.n

            t_plan = {}
            td = tb
            for et in range(NET):
                td += 1; t_plan[("rg", et)] = td
                td += 1; t_plan[("ig", et)] = td
                for g in range(4):
                    td += 1; t_plan[("xtr", et, g)] = td
                for g in range(4):
                    td += 1; t_plan[("htr", et, g)] = td
            s_plan = {}
            sd = sb
            for et in range(NET):
                sd += 1; s_plan[("u1", et)] = sd
                sd += 1; s_plan[("alpha", et)] = sd
                sd += 1; s_plan[("asq", et)] = sd
                sd += 1; s_plan[("lnw", et)] = sd
                sd += 1; s_plan[("beta", et)] = sd
                sd += 1; s_plan[("u2", et)] = sd
                for g in range(4):
                    sd += 1; s_plan[("hsq", et, g)] = sd
            v_plan = {}
            vd = vb
            for et in range(NET):
                vd += 1; v_plan[("wg", et)] = vd
                vd += 1; v_plan[("wi", et)] = vd
                vd += 1; v_plan[("w", et)] = vd
                for g in range(4):
                    vd += 1; v_plan[("xTevac", et, g)] = vd
                vd += 1; v_plan[("t1", et)] = vd
                vd += 1; v_plan[("xbeta", et)] = vd
                vd += 1; v_plan[("scan", et)] = vd
                for g in range(4):
                    vd += 1; v_plan[("hevac", et, g)] = vd

            def _qw(q):
                for et in range(NET):
                    for dd, wh, key in ((dWg, wgT_h, "wg"), (dWi, wiT_h, "wi")):
                        if et >= 1:
                            q.wait_ge(sV.sem, v_plan[(key, et - 1)])
                        m = 0 if key == "wg" else 1
                        q.dma_start(out=wst[:, m * 2048:(m + 1) * 2048],
                                    in_=wh[et]).then_inc(dd.sem, 16)
                        dd.n += 16
                    if gate_cut >= 4:
                        if et >= 1:
                            q.wait_ge(sT.sem, t_plan[("xtr", et - 1, 3)])
                        q.dma_start(out=xcol[:].rearrange("p (j c) -> p j c", j=16),
                                    in_=x_h[:, et * 128:(et + 1) * 128]
                                    .rearrange("(j p) c -> p j c", p=128)).then_inc(dXc.sem, 16)
                        dXc.n += 16
            block.sync(_qw)

            g_h = {}

            def _gh(g):
                if gate_cut < 6:
                    return
                for et in range(NET):
                    for grp in range(4):
                        dd = dH0 if grp % 2 == 0 else dH1
                        g.wait_ge(sV.sem, v_plan[("hevac", et, grp)])
                        par = (grp % 2) * 512
                        for j in range(4):
                            tt = grp * 4 + j
                            g.dma_start(
                                out=h_dr[tt * 128:(tt + 1) * 128, et * 128:(et + 1) * 128],
                                in_=hstg[:, par + j * 128:par + (j + 1) * 128]
                            ).then_inc(dd.sem, 16)
                            dd.n += 16
                        g_h[(et, grp)] = (dd, dd.n)
            block.gpsimd(_gh)

            def _tg(t):
                for et in range(NET):
                    t.wait_ge(sV.sem, v_plan[("wg", et)])
                    if et >= 1:
                        t.wait_ge(sS.sem, s_plan[("u2", et - 1)])
                    for d in range(16):
                        for nck in range(4):
                            t.matmul(pgg[:, nck * 512:(nck + 1) * 512],
                                     wg_bf[:, d * 128:(d + 1) * 128],
                                     xnT[:, d * 2048 + nck * 512: d * 2048 + (nck + 1) * 512],
                                     start=(d == 0), stop=(d == 15))
                    sT.inc(t.drain())
                    if gate_cut < 3:
                        continue
                    t.wait_ge(sV.sem, v_plan[("wi", et)])
                    t.wait_ge(sS.sem, s_plan[("u1", et)])
                    for d in range(16):
                        for nck in range(4):
                            t.matmul(pgg[:, nck * 512:(nck + 1) * 512],
                                     wi_bf[:, d * 128:(d + 1) * 128],
                                     xnT[:, d * 2048 + nck * 512: d * 2048 + (nck + 1) * 512],
                                     start=(d == 0), stop=(d == 15))
                    sT.inc(t.drain())
                    if gate_cut < 4:
                        continue
                    t.wait_ge(dXc.sem, 16 * (et + 1))
                    for g in range(4):
                        ps = ptr if g % 2 == 0 else ptr2
                        if et > 0 or g >= 2:
                            pet, pg = (et, g - 2) if g >= 2 else (et - 1, g + 2)
                            t.wait_ge(sV.sem, v_plan[("xTevac", pet, pg)])
                        for j in range(4):
                            jj = g * 4 + j
                            t.matmul(ps[:, j * 128:(j + 1) * 128],
                                     xcol[:, jj * 128:(jj + 1) * 128],
                                     idf[:], start=True, stop=True)
                        sT.inc(t.drain())
                    if gate_cut < 6:
                        continue
                    t.wait_ge(sV.sem, v_plan[("scan", et)])
                    for g in range(4):
                        ps = phh if g % 2 == 0 else phh2
                        if et > 0 or g >= 2:
                            pet, pg = (et, g - 2) if g >= 2 else (et - 1, g + 2)
                            t.wait_ge(sS.sem, s_plan[("hsq", pet, pg)])
                        for j in range(4):
                            jj = g * 4 + j
                            t.matmul(ps[:, j * 128:(j + 1) * 128],
                                     hTe[:, jj * 128:(jj + 1) * 128],
                                     idf[:], start=True, stop=True)
                        sT.inc(t.drain())
            block.tensor(_tg)
            assert gate_cut < 99 or sT.n == td

            def _sg(s):
                for et in range(NET):
                    s.wait_ge(sT.sem, t_plan[("rg", et)])
                    if et >= 1:
                        s.wait_ge(sV.sem, v_plan[("xbeta", et - 1)])
                    sS.inc(s.activation(bufA[:], pgg[:], AF.Sigmoid,
                                        bias=bgtc[:, et:et + 1]))
                    if gate_cut < 2:
                        continue
                    s.drain()
                    sS.inc(s.activation(bufC[:], bufA[:], AF.Exp,
                                        scale=negsp[:, et:et + 1]))
                    s.drain()
                    sS.inc(s.activation(bufA[:], bufC[:], AF.Square))
                    s.wait_ge(sV.sem, v_plan[("w", et)])
                    sS.inc(s.activation(bufA[:], bufB[:], AF.Ln))
                    s.drain()
                    sS.inc(s.activation(bufB[:], bufA[:], AF.Exp, scale=0.5))
                    if gate_cut < 3:
                        continue
                    s.wait_ge(sT.sem, t_plan[("ig", et)])
                    s.drain()
                    sS.inc(s.activation(bufA[:], pgg[:], AF.Sigmoid,
                                        bias=binc[:, et:et + 1]))
                    if gate_cut < 6:
                        continue
                    for g in range(4):
                        ps = phh if g % 2 == 0 else phh2
                        s.wait_ge(sV.sem, v_plan[("hevac", et, g)])
                        s.drain()
                        for j in range(4):
                            tt = g * 4 + j
                            s.activation(trash[:, j * 128:(j + 1) * 128],
                                         ps[:, j * 128:(j + 1) * 128],
                                         AF.Square,
                                         accum_out=sqh[:, tt * ND + et: tt * ND + et + 1])
                        sS.inc(s.drain())
            block.scalar(_sg)
            assert gate_cut < 99 or sS.n == sd

            def _vg(v):
                for et in range(NET):
                    v.wait_ge(dWg.sem, 16 * (et + 1))
                    if et >= 1:
                        v.wait_ge(sT.sem, t_plan[("rg", et - 1)])
                    sV.inc(v.tensor_copy(wg_bf[:], wst[:, 0:2048]))
                    v.wait_ge(dWi.sem, 16 * (et + 1))
                    if et >= 1:
                        v.wait_ge(sT.sem, t_plan[("ig", et - 1)])
                    sV.inc(v.tensor_copy(wi_bf[:], wst[:, 2048:4096]))
                    if gate_cut < 2:
                        continue
                    v.wait_ge(sS.sem, s_plan[("asq", et)])
                    sV.inc(v.tensor_scalar(bufB[:], bufA[:], 1.000001, -1.0,
                                           ALU.subtract, ALU.mult))
                    if gate_cut < 4:
                        continue
                    for g in range(4):
                        ps = ptr if g % 2 == 0 else ptr2
                        v.wait_ge(sT.sem, t_plan[("xtr", et, g)])
                        sV.inc(v.tensor_copy(xTe[:, g * 512:(g + 1) * 512], ps[:]))
                    if gate_cut < 5:
                        continue
                    v.wait_ge(sS.sem, s_plan[("beta", et)])
                    v.drain()
                    sV.inc(v.tensor_tensor(xTe[:], bufB[:], xTe[:], ALU.mult))
                    v.wait_ge(sS.sem, s_plan[("u2", et)])
                    v.drain()
                    sV.inc(v.tensor_tensor(xTe[:], xTe[:], bufA[:], ALU.mult))
                    if et >= 1:
                        v.wait_ge(sT.sem, t_plan[("htr", et - 1, 3)])
                    v.drain()
                    sV.inc(v.tensor_tensor_scan(hTe[:], bufC[:], xTe[:], 0.0,
                                                ALU.mult, ALU.add))
                    if gate_cut < 6:
                        continue
                    for g in range(4):
                        ps = phh if g % 2 == 0 else phh2
                        v.wait_ge(sT.sem, t_plan[("htr", et, g)])
                        if et > 0 or g >= 2:
                            pet, pg = (et, g - 2) if g >= 2 else (et - 1, g + 2)
                            hdd, hdn = g_h[(pet, pg)]
                            v.wait_ge(hdd.sem, hdn)
                        par = (g % 2) * 512
                        v.tensor_copy(hstg[:, par:par + 512], ps[:])
                        for j in range(4):
                            tt = g * 4 + j
                            v.reduce_sum(sumh[:, tt * ND + et: tt * ND + et + 1],
                                         ps[:, j * 128:(j + 1) * 128], axis=AX)
                        sV.inc(v.drain())
            block.vector(_vg)
            assert gate_cut < 99 or sV.n == vd

            barrier()

            if stop_after == "gates":
                return nc
            v_hst = sV.n + 1
            s_hr = sS.n + 1
            v_res = [v_hst + 1 + tt for tt in range(NT)]

            rb0 = dB0.n
            rb1 = dB1.n
            ob = out_ctr.n

            def _gr(g):
                for tt in range(NT):
                    d = dB0 if tt % 2 == 0 else dB1
                    if tt >= 2:
                        g.wait_ge(sV.sem, v_res[tt - 2])
                        g.wait_ge(out_ctr.sem, ob + 16 * (tt - 1))
                    g.dma_start(out=xst[:, (tt % 2) * 2048:(tt % 2 + 1) * 2048],
                                in_=x_h[tt * 128:(tt + 1) * 128, :]).then_inc(d.sem, 16)
                    d.n += 16
                    if tt >= 1:
                        g.wait_ge(sV.sem, v_res[tt - 1])
                        if tt >= 2:
                            g.wait_ge(out_ctr.sem, out_ctr.n)
                        bb = ((tt - 1) % 2) * 2048
                        g.dma_start(out=out_dst[(tt - 1) * 128:tt * 128, :],
                                    in_=xst[:, bb:bb + 2048]).then_inc(out_ctr.sem, 16)
                        out_ctr.n += 16
                g.wait_ge(sV.sem, v_res[NT - 1])
                g.wait_ge(out_ctr.sem, out_ctr.n)
                bb = ((NT - 1) % 2) * 2048
                g.dma_start(out=out_dst[(NT - 1) * 128:NT * 128, :],
                            in_=xst[:, bb:bb + 2048]).then_inc(out_ctr.sem, 16)
                out_ctr.n += 16
            block.gpsimd(_gr)

            ra0 = dA0.n
            ra1 = dA1.n

            def _qh(q):
                q.wait_ge(dH0.sem, dH0.n)
                q.wait_ge(dH1.sem, dH1.n)
                for tt in range(NT):
                    d = dA0 if tt % 2 == 0 else dA1
                    if tt >= 2:
                        q.wait_ge(sV.sem, v_res[tt - 2])
                    q.dma_start(out=xnst[:, (tt % 2) * 2048:(tt % 2 + 1) * 2048],
                                in_=h_dr[tt * 128:(tt + 1) * 128, :]).then_inc(d.sem, 16)
                    d.n += 16
            block.sync(_qh)

            def _vst(v):
                for tt in range(NT):
                    v.reduce_sum(cst[:, tt:tt + 1], sumh[:, tt * ND:(tt + 1) * ND], axis=AX)
                    v.reduce_sum(cst[:, 32 + tt:32 + tt + 1],
                                 sqh[:, tt * ND:(tt + 1) * ND], axis=AX)
                v.drain()
                v.tensor_scalar_mul(mcol[:, 0:NT], cst[:, 0:NT], 1.0 / D)
                v.tensor_scalar_mul(cst[:, 32:32 + NT], cst[:, 32:32 + NT], 1.0 / D)
                v.drain()
                v.tensor_tensor(cst[:, 0:NT], mcol[:, 0:NT], mcol[:, 0:NT], ALU.mult)
                v.drain()
                v.tensor_sub(cst[:, 32:32 + NT], cst[:, 32:32 + NT], cst[:, 0:NT])
                v.drain()
                sV.inc(v.tensor_scalar_add(cst[:, 32:32 + NT], cst[:, 32:32 + NT], EPS))
            block.vector(_vst)
            assert sV.n == v_hst

            def _sst(s):
                s.wait_ge(sV.sem, v_hst)
                s.activation(cst[:, 32:32 + NT], cst[:, 32:32 + NT], AF.Ln)
                s.drain()
                sS.inc(s.activation(rcol[:, 0:NT], cst[:, 32:32 + NT], AF.Exp, scale=-0.5))
            block.scalar(_sst)
            assert sS.n == s_hr

            def _vres(v):
                v.wait_ge(sS.sem, s_hr)
                for tt in range(NT):
                    v.wait_ge((dB0 if tt % 2 == 0 else dB1).sem,
                              (rb0 if tt % 2 == 0 else rb1) + 16 * (tt // 2 + 1))
                    v.wait_ge((dA0 if tt % 2 == 0 else dA1).sem,
                              (ra0 if tt % 2 == 0 else ra1) + 16 * (tt // 2 + 1))
                    b = (tt % 2) * 2048
                    if tt > 0:
                        v.drain()
                    v.tensor_scalar(bufA[:], xnst[:, b:b + 2048],
                                    mcol[:, tt:tt + 1], rcol[:, tt:tt + 1],
                                    ALU.subtract, ALU.mult)
                    v.drain()
                    sV.inc(v.tensor_add(xst[:, b:b + 2048], xst[:, b:b + 2048], bufA[:]))
            block.vector(_vres)
            assert sV.n == v_res[-1]

            barrier()


        if n_phases >= 2:
            out_dst2 = rB if n_phases > 2 else y_h
            out_ctr2 = dRA if n_phases > 2 else sO
            with ExitStack() as ph:
                pe = ph.enter_context
                kq = pe(nc.sbuf_tensor("kq", [128, ND * 2048], BF16))
                rst = pe(nc.sbuf_tensor("rst2", [128, 2 * 2048], F32))
                bufA = pe(nc.sbuf_tensor("bufA2", [128, 2048], F32))
                bufB = pe(nc.sbuf_tensor("bufB2", [128, 2048], F32))
                qbf = pe(nc.sbuf_tensor("qbf2", [128, 2048], BF16))
                qTc = pe(nc.sbuf_tensor("qTc2", [128, 2 * 2048], BF16))
                skB = pe(nc.sbuf_tensor("skB2", [128, 2048], F32))
                bqB = pe(nc.sbuf_tensor("bqB2", [128, 2048], F32))
                skcol = pe(nc.sbuf_tensor("skcol2", [128, 16], F32))
                skrow = pe(nc.sbuf_tensor("skrow2", [1, 2048], F32))
                skinv = pe(nc.sbuf_tensor("skinv2", [1, 2048], F32))
                skBi = pe(nc.sbuf_tensor("skBi2", [128, 2048], F32))
                brow = pe(nc.sbuf_tensor("brow2", [1, 2048], F32))
                bq = pe(nc.sbuf_tensor("bq2", [1, 2048], F32))
                ones1 = pe(nc.sbuf_tensor("ones2", [1, 128], F32))
                cs = pe(nc.sbuf_tensor("cs2", [128, 128], F32))
                trash2 = pe(nc.sbuf_tensor("trash2", [128, 2048], BF16))
                ptr = pe(nc.psum_tensor("ptrF", [128, 512], F32))
                ptr2 = pe(nc.psum_tensor("ptr2F", [128, 512], F32))
                py = pe(nc.psum_tensor("pyF", [128, 2048], F32))

                tb, vb, sb = sT.n, sV.n, sS.n
                v_abs = [vb + i + 1 for i in range(NT)]
                v_fold = [vb + NT + g + 1 for g in range(4)]
                v_skrow = vb + NT + 5
                v_bq = vb + NT + 6
                v_skb = vb + NT + 7
                v_ski = vb + NT + 8
                v_bqb = vb + NT + 9
                v_kq = [vb + NT + 9 + i + 1 for i in range(NT)]
                vpre_end = vb + NT + 9 + NT
                t_fold = [tb + g + 1 for g in range(4)]
                t_skb = tb + 5
                t_ski = tb + 6
                t_bqb = tb + 7
                tpre_end = tb + 7
                v_plan2 = {}
                vd = vpre_end
                for tt in range(NT):
                    vd += 1; v_plan2[("rsum", tt)] = vd
                    vd += 1; v_plan2[("var1", tt)] = vd
                    vd += 1; v_plan2[("q", tt)] = vd
                    for g in range(4):
                        vd += 1; v_plan2[("qT", tt, g)] = vd
                    vd += 1; v_plan2[("ep1", tt)] = vd
                    vd += 1; v_plan2[("ep2", tt)] = vd
                    vd += 1; v_plan2[("var2", tt)] = vd
                    vd += 1; v_plan2[("out", tt)] = vd
                s_plan2 = {}
                sd = sb + NT
                for tt in range(NT):
                    sd += 1; s_plan2[("sq1", tt)] = sd
                    sd += 1; s_plan2[("rstd1", tt)] = sd
                    sd += 1; s_plan2[("sq2", tt)] = sd
                    sd += 1; s_plan2[("rstd2", tt)] = sd
                t_plan2 = {}
                td = tpre_end
                for tt in range(NT):
                    for g in range(4):
                        td += 1; t_plan2[("q", tt, g)] = td
                    td += 1; t_plan2[("mm", tt)] = td

                ka0 = dA0.n
                ka1 = dA1.n

                s_abs = [sb + i + 1 for i in range(NT)]

                def _sabs(s):
                    for i in range(NT):
                        s.wait_ge((dA0 if i % 2 == 0 else dA1).sem,
                                  (ka0 if i % 2 == 0 else ka1) + 16 * (i // 2 + 1))
                        if i >= 1:
                            s.wait_ge(sV.sem, v_abs[i - 1])
                            s.drain()
                        b = (i % 2) * 2048
                        sS.inc(s.activation(bufB[:], rst[:, b:b + 2048], AF.Abs))
                block.scalar(_sabs)

                def _qk1(q):
                    q.wait_ge(dRA.sem, dRA.n)
                    if n_phases == 2:
                        pass
                    q.wait_ge(sO.sem, sO.n)
                    for i in range(NT):
                        d = dA0 if i % 2 == 0 else dA1
                        if i >= 2:
                            q.wait_ge(sV.sem, v_abs[i - 2])
                        q.dma_start(out=rst[:, (i % 2) * 2048:(i % 2 + 1) * 2048],
                                    in_=fker_h[i * 128:(i + 1) * 128, :]).then_inc(d.sem, 16)
                        d.n += 16
                    q.dma_start(out=brow[:], in_=fbias_h[:]).then_inc(dC.sem, 16)
                    dC.n += 16
                block.sync(_qk1)
                qb_c = dC.n

                def _vk1(v):
                    v.memset(ones1[:], 1.0)
                    for i in range(NT):
                        v.wait_ge(sS.sem, s_abs[i])
                        if i == 0:
                            sV.inc(v.tensor_copy(bufA[:], bufB[:]))
                        else:
                            v.drain()
                            sV.inc(v.tensor_max(bufA[:], bufA[:], bufB[:]))
                    for g in range(4):
                        ps = ptr if g % 2 == 0 else ptr2
                        v.wait_ge(sT.sem, t_fold[g])
                        for j in range(4):
                            ft = g * 4 + j
                            v.reduce_max(skcol[:, ft:ft + 1], ps[:, j * 128:(j + 1) * 128],
                                         axis=AX)
                        sV.inc(v.drain())
                block.vector(_vk1)

                def _tk1(t):
                    t.wait_ge(sV.sem, v_abs[NT - 1])
                    for g in range(4):
                        ps = ptr if g % 2 == 0 else ptr2
                        if g >= 2:
                            t.wait_ge(sV.sem, v_fold[g - 2])
                        for j in range(4):
                            ft = g * 4 + j
                            t.matmul(ps[:, j * 128:(j + 1) * 128],
                                     bufA[:, ft * 128:(ft + 1) * 128],
                                     idf[:], start=True, stop=True)
                        sT.inc(t.drain())
                block.tensor(_tk1)

                gk0 = dH0.n

                def _gk(g):
                    g.wait_ge(sV.sem, v_fold[3])
                    g.dma_start(out=skd[0, :].rearrange("(j p) -> p j", p=128),
                                in_=skcol[:],
                                allow_slow_non_contiguous=True).then_inc(dH0.sem, 16)
                    dH0.n += 16
                block.gpsimd(_gk)

                def _qk2(q):
                    q.wait_ge(dH0.sem, gk0 + 16)
                    q.dma_start(out=skrow[:], in_=skd[:]).then_inc(dXc.sem, 16)
                    dXc.n += 16
                block.sync(_qk2)
                qsk_c = dXc.n

                def _vk2(v):
                    v.wait_ge(dXc.sem, qsk_c)
                    v.tensor_scalar_max(skrow[:], skrow[:], 1e-30)
                    v.drain()
                    v.reciprocal(skinv[:], skrow[:])
                    v.drain()
                    v.tensor_scalar_mul(skinv[:], skinv[:], 127.0)
                    v.drain()
                    sV.inc(v.tensor_scalar_mul(skrow[:], skrow[:], 1.0 / 127.0))
                    v.wait_ge(dC.sem, qb_c)
                    v.reduce_max(cs[0:1, 0:1], brow[:], axis=AX,
                                 apply_absolute_value=True)
                    v.drain()
                    v.tensor_scalar_max(cs[0:1, 0:1], cs[0:1, 0:1], 1e-30)
                    v.drain()
                    v.reciprocal(cs[0:1, 1:2], cs[0:1, 0:1])
                    v.drain()
                    v.tensor_scalar(cs[0:1, 0:1], cs[0:1, 0:1], 1.0 / 127.0, None, ALU.mult)
                    v.tensor_scalar(cs[0:1, 1:2], cs[0:1, 1:2], 127.0, None, ALU.mult)
                    v.drain()
                    v.tensor_scalar(bq[:], brow[:], cs[0:1, 1:2], None, ALU.mult)
                    v.drain()
                    v.tensor_scalar(bq[:], bq[:], MAGIC, MAGIC, ALU.add, ALU.subtract)
                    v.drain()
                    v.tensor_scalar(bq[:], bq[:], 127.0, -127.0, ALU.min, ALU.max)
                    v.drain()
                    sV.inc(v.tensor_scalar(bq[:], bq[:], cs[0:1, 0:1], None, ALU.mult))
                    v.wait_ge(sT.sem, t_skb)
                    sV.inc(v.tensor_copy(skB[:], py[:]))
                    v.wait_ge(sT.sem, t_ski)
                    sV.inc(v.tensor_copy(skBi[:], py[:]))
                    v.wait_ge(sT.sem, t_bqb)
                    sV.inc(v.tensor_copy(bqB[:], py[:]))
                block.vector(_vk2)

                def _tk2(t):
                    t.wait_ge(sV.sem, v_skrow)
                    for nck in range(4):
                        t.matmul(py[:, nck * 512:(nck + 1) * 512], ones1[:],
                                 skrow[0:1, nck * 512:(nck + 1) * 512],
                                 start=True, stop=True)
                    sT.inc(t.drain())
                    t.wait_ge(sV.sem, v_skb)
                    for nck in range(4):
                        t.matmul(py[:, nck * 512:(nck + 1) * 512], ones1[:],
                                 skinv[0:1, nck * 512:(nck + 1) * 512],
                                 start=True, stop=True)
                    sT.inc(t.drain())
                    t.wait_ge(sV.sem, v_ski)
                    t.wait_ge(sV.sem, v_bq)
                    for nck in range(4):
                        t.matmul(py[:, nck * 512:(nck + 1) * 512], ones1[:],
                                 bq[0:1, nck * 512:(nck + 1) * 512],
                                 start=True, stop=True)
                    sT.inc(t.drain())
                block.tensor(_tk2)

                kb0 = dA0.n
                kb1 = dA1.n

                def _qk3(q):
                    for i in range(NT):
                        d = dA0 if i % 2 == 0 else dA1
                        if i >= 2:
                            q.wait_ge(sV.sem, v_kq[i - 2])
                        else:
                            q.wait_ge(sV.sem, v_abs[NT - 1])
                        q.dma_start(out=rst[:, (i % 2) * 2048:(i % 2 + 1) * 2048],
                                    in_=fker_h[i * 128:(i + 1) * 128, :]).then_inc(d.sem, 16)
                        d.n += 16
                block.sync(_qk3)

                def _vk3(v):
                    v.wait_ge(sV.sem, v_bqb)
                    for i in range(NT):
                        v.wait_ge((dA0 if i % 2 == 0 else dA1).sem,
                                  (kb0 if i % 2 == 0 else kb1) + 16 * (i // 2 + 1))
                        b = (i % 2) * 2048
                        v.drain()
                        v.tensor_tensor(bufB[:], rst[:, b:b + 2048], skBi[:], ALU.mult)
                        v.drain()
                        v.tensor_scalar(bufB[:], bufB[:], MAGIC, MAGIC, ALU.add, ALU.subtract)
                        v.drain()
                        sV.inc(v.tensor_scalar(kq[:, i * 2048:(i + 1) * 2048], bufB[:],
                                               127.0, -127.0, ALU.min, ALU.max))
                block.vector(_vk3)
                assert sV.n == vpre_end, (sV.n, vpre_end)
                assert sT.n == tpre_end

                barrier()

                ma0, ma1 = dA2.n, dA3.n
                ob2 = out_ctr2.n

                def _qm(q):
                    q.wait_ge(dRA.sem, dRA.n)
                    for tt in range(NT):
                        d = dA2 if tt % 2 == 0 else dA3
                        if tt >= 2:
                            q.wait_ge(sV.sem, v_plan2[("out", tt - 2)])
                            q.wait_ge(out_ctr2.sem, ob2 + 16 * (tt - 1))
                        q.dma_start(out=rst[:, (tt % 2) * 2048:(tt % 2 + 1) * 2048],
                                    in_=rA[tt * 128:(tt + 1) * 128, :]).then_inc(d.sem, 16)
                        d.n += 16
                block.sync(_qm)

                def _gm(g):
                    for tt in range(NT):
                        g.wait_ge(sV.sem, v_plan2[("out", tt)])
                        if tt >= 1:
                            g.wait_ge(out_ctr2.sem, out_ctr2.n)
                        b = (tt % 2) * 2048
                        g.dma_start(out=out_dst2[tt * 128:(tt + 1) * 128, :],
                                    in_=rst[:, b:b + 2048]).then_inc(out_ctr2.sem, 16)
                        out_ctr2.n += 16
                block.gpsimd(_gm)

                def _sm(s):
                    for tt in range(NT):
                        b = (tt % 2) * 2048
                        s.wait_ge((dA2 if tt % 2 == 0 else dA3).sem,
                                  (ma0 if tt % 2 == 0 else ma1) + 16 * (tt // 2 + 1))
                        s.drain()
                        sS.inc(s.activation(trash2[:], rst[:, b:b + 2048], AF.Square,
                                            accum_out=cs[:, 32 + tt:32 + tt + 1]))
                        s.wait_ge(sV.sem, v_plan2[("var1", tt)])
                        s.drain()
                        s.activation(cs[:, 48 + tt:48 + tt + 1], cs[:, 48 + tt:48 + tt + 1],
                                     AF.Ln)
                        s.drain()
                        sS.inc(s.activation(cs[:, 64 + tt:64 + tt + 1],
                                            cs[:, 48 + tt:48 + tt + 1], AF.Exp, scale=-0.5))
                        s.wait_ge(sV.sem, v_plan2[("ep2", tt)])
                        s.drain()
                        sS.inc(s.activation(trash2[:], bufB[:], AF.Square,
                                            accum_out=cs[:, 80 + tt:80 + tt + 1]))
                        s.wait_ge(sV.sem, v_plan2[("var2", tt)])
                        s.drain()
                        s.activation(cs[:, 96 + tt:96 + tt + 1], cs[:, 96 + tt:96 + tt + 1],
                                     AF.Ln)
                        s.drain()
                        sS.inc(s.activation(cs[:, 112 + tt:112 + tt + 1],
                                            cs[:, 96 + tt:96 + tt + 1], AF.Exp, scale=-0.5))
                block.scalar(_sm)
                assert sS.n == sd

                def _tm(t):
                    for tt in range(NT):
                        t.wait_ge(sV.sem, v_plan2[("q", tt)])
                        for g in range(4):
                            ps = ptr if g % 2 == 0 else ptr2
                            k = tt * 4 + g
                            if k >= 2:
                                pt, pg = divmod(k - 2, 4)
                                t.wait_ge(sV.sem, v_plan2[("qT", pt, pg)])
                            for j in range(4):
                                dt = g * 4 + j
                                t.matmul(ps[:, j * 128:(j + 1) * 128],
                                         qbf[:, dt * 128:(dt + 1) * 128],
                                         idb[:], start=True, stop=True)
                            sT.inc(t.drain())
                        t.wait_ge(sV.sem, v_plan2[("qT", tt, 3)])
                        if tt >= 1:
                            t.wait_ge(sV.sem, v_plan2[("ep1", tt - 1)])
                        bq_par = (tt % 2) * 2048
                        for dd in range(16):
                            for nck in range(4):
                                t.matmul(py[:, nck * 512:(nck + 1) * 512],
                                         qTc[:, bq_par + dd * 128:bq_par + (dd + 1) * 128],
                                         kq[:, dd * 2048 + nck * 512:dd * 2048 + (nck + 1) * 512],
                                         start=(dd == 0), stop=(dd == 15))
                        sT.inc(t.drain())
                block.tensor(_tm)
                assert sT.n == td

                def _vm(v):
                    for tt in range(NT):
                        b = (tt % 2) * 2048
                        v.wait_ge((dA2 if tt % 2 == 0 else dA3).sem,
                                  (ma0 if tt % 2 == 0 else ma1) + 16 * (tt // 2 + 1))
                        sV.inc(v.reduce_sum(cs[:, tt:tt + 1], rst[:, b:b + 2048], axis=AX))
                        v.wait_ge(sS.sem, s_plan2[("sq1", tt)])
                        v.drain()
                        v.tensor_scalar_mul(cs[:, 16 + tt:16 + tt + 1], cs[:, tt:tt + 1],
                                            1.0 / D)
                        v.tensor_scalar_mul(cs[:, 32 + tt:32 + tt + 1],
                                            cs[:, 32 + tt:32 + tt + 1], 1.0 / D)
                        v.drain()
                        v.tensor_tensor(cs[:, 48 + tt:48 + tt + 1], cs[:, 16 + tt:16 + tt + 1],
                                        cs[:, 16 + tt:16 + tt + 1], ALU.mult)
                        v.drain()
                        v.tensor_sub(cs[:, 48 + tt:48 + tt + 1], cs[:, 32 + tt:32 + tt + 1],
                                     cs[:, 48 + tt:48 + tt + 1])
                        v.drain()
                        sV.inc(v.tensor_scalar_add(cs[:, 48 + tt:48 + tt + 1],
                                                   cs[:, 48 + tt:48 + tt + 1], EPS))
                        v.wait_ge(sS.sem, s_plan2[("rstd1", tt)])
                        v.tensor_scalar(bufA[:], rst[:, b:b + 2048],
                                        cs[:, 16 + tt:16 + tt + 1], cs[:, 64 + tt:64 + tt + 1],
                                        ALU.subtract, ALU.mult)
                        v.drain()
                        v.reduce_max(cs[:, 0 + tt:0 + tt + 1], bufA[:], axis=AX,
                                     apply_absolute_value=True)
                        v.drain()
                        v.tensor_scalar_max(cs[:, tt:tt + 1], cs[:, tt:tt + 1], 1e-30)
                        v.drain()
                        v.reciprocal(cs[:, 112 + tt:112 + tt + 1], cs[:, tt:tt + 1])
                        v.tensor_scalar(cs[:, 16 + tt:16 + tt + 1], cs[:, tt:tt + 1],
                                        1.0 / 127.0, None, ALU.mult)
                        v.drain()
                        v.tensor_scalar(cs[:, 112 + tt:112 + tt + 1],
                                        cs[:, 112 + tt:112 + tt + 1], 127.0, None,
                                        ALU.mult)
                        v.drain()
                        v.tensor_scalar(bufA[:], bufA[:], cs[:, 112 + tt:112 + tt + 1],
                                        None, ALU.mult)
                        v.drain()
                        v.tensor_scalar(bufA[:], bufA[:], MAGIC, MAGIC, ALU.add, ALU.subtract)
                        v.drain()
                        if tt >= 2:
                            v.wait_ge(sT.sem, t_plan2[("q", tt - 2, 3)])
                        sV.inc(v.tensor_scalar(qbf[:], bufA[:], 127.0, -127.0,
                                               ALU.min, ALU.max))
                        for g in range(4):
                            ps = ptr if g % 2 == 0 else ptr2
                            v.wait_ge(sT.sem, t_plan2[("q", tt, g)])
                            if tt >= 2 and g == 0:
                                v.wait_ge(sT.sem, t_plan2[("mm", tt - 2)])
                            sV.inc(v.tensor_copy(qTc[:, b + g * 512:b + (g + 1) * 512], ps[:]))
                        v.wait_ge(sT.sem, t_plan2[("mm", tt)])
                        sV.inc(v.scalar_tensor_tensor(bufA[:], py[:],
                                                      cs[:, 16 + tt:16 + tt + 1], skB[:],
                                                      ALU.mult, ALU.mult))
                        v.drain()
                        v.tensor_add(bufA[:], bufA[:], bqB[:])
                        v.drain()
                        v.tensor_scalar_max(bufA[:], bufA[:], 0.0)
                        v.drain()
                        v.reduce_max(cs[:, 0 + tt:0 + tt + 1], bufA[:], axis=AX)
                        v.drain()
                        v.tensor_scalar_max(cs[:, tt:tt + 1], cs[:, tt:tt + 1], 1e-30)
                        v.drain()
                        v.reciprocal(cs[:, 112 + tt:112 + tt + 1], cs[:, tt:tt + 1])
                        v.tensor_scalar(cs[:, 16 + tt:16 + tt + 1], cs[:, tt:tt + 1],
                                        1.0 / 127.0, None, ALU.mult)
                        v.drain()
                        v.tensor_scalar(cs[:, 112 + tt:112 + tt + 1],
                                        cs[:, 112 + tt:112 + tt + 1], 127.0, None, ALU.mult)
                        v.drain()
                        v.tensor_scalar(bufB[:], bufA[:], cs[:, 112 + tt:112 + tt + 1],
                                        None, ALU.mult)
                        v.drain()
                        v.tensor_scalar(bufB[:], bufB[:], MAGIC, MAGIC, ALU.add, ALU.subtract)
                        v.drain()
                        v.tensor_scalar(bufB[:], bufB[:], 127.0, -127.0, ALU.min, ALU.max)
                        v.drain()
                        v.tensor_scalar(bufB[:], bufB[:], cs[:, 16 + tt:16 + tt + 1], None,
                                        ALU.mult)
                        v.drain()
                        sV.inc(v.reduce_sum(cs[:, tt:tt + 1], bufB[:], axis=AX))
                        v.wait_ge(sS.sem, s_plan2[("sq2", tt)])
                        v.drain()
                        v.tensor_scalar_mul(cs[:, 64 + tt:64 + tt + 1], cs[:, tt:tt + 1],
                                            1.0 / D)
                        v.tensor_scalar_mul(cs[:, 80 + tt:80 + tt + 1],
                                            cs[:, 80 + tt:80 + tt + 1], 1.0 / D)
                        v.drain()
                        v.tensor_tensor(cs[:, 96 + tt:96 + tt + 1], cs[:, 64 + tt:64 + tt + 1],
                                        cs[:, 64 + tt:64 + tt + 1], ALU.mult)
                        v.drain()
                        v.tensor_sub(cs[:, 96 + tt:96 + tt + 1], cs[:, 80 + tt:80 + tt + 1],
                                     cs[:, 96 + tt:96 + tt + 1])
                        v.drain()
                        sV.inc(v.tensor_scalar_add(cs[:, 96 + tt:96 + tt + 1],
                                                   cs[:, 96 + tt:96 + tt + 1], EPS))
                        v.wait_ge(sS.sem, s_plan2[("rstd2", tt)])
                        v.tensor_scalar(bufA[:], bufB[:], cs[:, 64 + tt:64 + tt + 1],
                                        cs[:, 112 + tt:112 + tt + 1], ALU.subtract, ALU.mult)
                        v.drain()
                        sV.inc(v.tensor_add(rst[:, b:b + 2048], rst[:, b:b + 2048], bufA[:]))
                block.vector(_vm)
                assert sV.n == vd, (sV.n, vd)

                barrier()

        p3_store_marks = []
        if n_phases >= 3:
            jstate = {"n": 0, "sec": 0, "bufs": None, "L": {}, "marks": []}

            def bg_alloc(pe):
                t = jstate["sec"]
                jstate["sec"] += 1
                try:
                    jstate["bufs"] = (
                        pe(nc.sbuf_tensor("jbi%d" % t, [128, 2752], I32)),
                        pe(nc.sbuf_tensor("jbf%d" % t, [128, 2752], BF16)),
                        pe(nc.sbuf_tensor("jmsk%d" % t, [128, 2752], BF16)),
                        pe(nc.sbuf_tensor("jacc%d" % t, [128, 2 * 2752], BF16)),
                    )
                except AssertionError:
                    jstate["bufs"] = None
                jstate["L"] = {}

            def bg_pump(g, k):
                if jstate["bufs"] is None:
                    return
                jbi, jbf, jmsk, jacc = jstate["bufs"]
                L = jstate["L"]
                for _ in range(k):
                    j = jstate["n"]
                    if j >= 12:
                        return
                    jstate["n"] += 1
                    sl, hf = divmod(j, 2)

                    def fjq(h, sl=sl, hf=hf, jbi=jbi):
                        return [h.dma_start(
                            out=jbi[:],
                            in_=myidx_h[sl][:, hf * 2752:(hf + 1) * 2752])]
                    jq = g.add("Q", fjq, predeps=[L["c"]] if "c" in L else ())

                    def fjc(h, jbi=jbi, jbf=jbf):
                        h.copy(out=jbf[:], in_=jbi[:])
                    jc = g.add("S", fjc, deps=[jq],
                               predeps=[L["d"]] if "d" in L else ())

                    def fjd(h, j=j, jbf=jbf, jmsk=jmsk, jacc=jacc):
                        b = (j % 2) * 2752
                        dec16g(h, lutb_sb, jacc[:, b:b + 2752], jbf, jmsk, 2752)
                    jd = g.add("V", fjd, deps=[jc],
                               predeps=[m for m in (L.get("g2"),)
                                        if m is not None])

                    def fjg(h, j=j, sl=sl, hf=hf, jacc=jacc):
                        b = (j % 2) * 2752
                        return [h.dma_start(
                            out=agsrc[sl][:, hf * 2752:(hf + 1) * 2752],
                            in_=jacc[:, b:b + 2752])]
                    jg = g.add("G", fjg, deps=[jd])
                    L["c"] = g.steps[jc][3]
                    L["d"] = g.steps[jd][3]
                    L["g2"] = L.get("g1")
                    L["g1"] = g.steps[jg][3]
                    jstate["marks"].append(g.steps[jg][3])

            class _C:
                pass
            c = _C()
            c.bg_alloc = bg_alloc
            c.bg_pump = bg_pump
            for k_, v_ in dict(
                nc=nc, block=block, barrier=barrier, n_phases=n_phases,
                sT=sT, sV=sV, sS=sS, sO=sO, sP=sP, dRA=dRA, dA0=dA0, dA1=dA1,
                dA2=dA2, dA3=dA3, dB0=dB0, dB1=dB1, dWg=dWg, dWi=dWi,
                dXc=dXc, dH0=dH0, dH1=dH1, dC=dC,
                dQa=dQa, dQb=dQb, dGa=dGa, dGb=dGb,
                idf=idf, idb=idb, rA=rA, rB=rB, y_h=y_h,
                wq_h=wq_h, wkvk_h=wkvk_h, wkvv_h=wkvv_h, wo_h=wo_h,
                w1_h=w1_h, b1c_h=b1c_h, w2c_h=w2c_h, b2s_h=b2s_h,
                csq_h=csq_h, ssq_h=ssq_h, ktcs_h=ktcs_h, ktss_h=ktss_h,
                iotc_h=iotc_h, perm_h=perm_h, outdr=outdr, xndr=xndr,
                lutb_h=lutb_h, myidx_h=myidx_h, hTdr=hTdr,
                agsrc=agsrc, agdst=agdst,
            ).items():
                setattr(c, k_, v_)
            marks3 = emit_phase3(c)
            if n_phases >= 4:
                c.p3_marks = marks3
                c.jg_marks = jstate["marks"]
                p3_store_marks.extend(emit_phase4(c))
            else:
                p3_store_marks.extend(marks3)

        def _fin(q):
            q.wait_ge(sO.sem, sO.n)
            for mctr, mcnt in p3_store_marks:
                q.wait_ge(mctr.sem, mcnt)
        block.sync(_fin)

    return nc


def _prep_inputs(inputs):
    import ml_dtypes
    sh = {}
    wgT = np.ascontiguousarray(inputs["L0_attn_W_gate"].T)
    wiT = np.ascontiguousarray(inputs["L0_attn_W_in"].T)
    sh["wgT"] = np.ascontiguousarray(
        wgT.reshape(16, 128, 16, 128).transpose(2, 1, 0, 3)).reshape(16, 128, 2048)
    sh["wiT"] = np.ascontiguousarray(
        wiT.reshape(16, 128, 16, 128).transpose(2, 1, 0, 3)).reshape(16, 128, 2048)
    sh["binc"] = np.ascontiguousarray(inputs["L0_attn_b_in"].reshape(16, 128).T)
    sh["bgtc"] = np.ascontiguousarray(inputs["L0_attn_b_gate"].reshape(16, 128).T)
    sh["lamc"] = np.ascontiguousarray(inputs["L0_attn_forget_lambda"].reshape(16, 128).T)
    sh["idf"] = np.eye(128, dtype=np.float32)
    sh["idb"] = np.eye(128, dtype=np.float32).astype(ml_dtypes.bfloat16)
    sh["fker"] = np.ascontiguousarray(inputs["L0_ffn_kernel"], dtype=np.float32)
    sh["fbias"] = np.ascontiguousarray(inputs["L0_ffn_bias"], dtype=np.float32).reshape(1, -1)
    Wq = inputs["L1_attn_Wq"].astype(np.float32)
    sh["wqt"] = np.ascontiguousarray(
        Wq.reshape(16, 128, 16, 128).transpose(2, 1, 0, 3)).reshape(16, 128, 2048)
    Wkv = inputs["L1_attn_Wkv"].astype(np.float32)
    wk = Wkv[:, :512]
    sh["wkvk"] = np.ascontiguousarray(
        wk.reshape(16, 128, 4, 128).transpose(2, 1, 0, 3)).reshape(4, 128, 2048)
    sh["wkvv"] = np.ascontiguousarray(Wkv[:, 512:])
    sh["wo"] = np.ascontiguousarray(inputs["L1_attn_Wo"], dtype=np.float32)
    W1 = inputs["L1_attn_W1"].astype(np.float32)
    sh["w1t"] = np.ascontiguousarray(
        W1.reshape(16, 128, 4, 128).transpose(2, 1, 0, 3)).reshape(4, 128, 2048)
    sh["b1c"] = np.ascontiguousarray(inputs["L1_attn_b1"].reshape(4, 128).T.astype(np.float32))
    sh["w2c"] = np.ascontiguousarray(inputs["L1_attn_W2"].reshape(4, 128).T.astype(np.float32))
    sh["b2s"] = np.ascontiguousarray(inputs["L1_attn_b2"].reshape(1, 1).astype(np.float32))
    HD = 128
    inv = 1.0 / (10000.0 ** (np.arange(0, HD, 2, dtype=np.float64) / HD))
    t_ = np.arange(2048, dtype=np.float64)
    ang_q = t_[None, :] * inv[:, None]
    cs = np.cos(ang_q)
    sn = np.sin(ang_q)
    sh["csq"] = np.ascontiguousarray(np.concatenate([cs, cs], 0).astype(np.float32))
    sh["ssq"] = np.ascontiguousarray(np.concatenate([-sn, sn], 0).astype(np.float32))
    ang_l = t_[:, None] * inv[None, :]
    csl = np.cos(ang_l)
    snl = np.sin(ang_l)
    ktcs = np.concatenate([csl, csl], 1).astype(np.float32)
    ktss = np.concatenate([-snl, snl], 1).astype(np.float32)
    sh["ktcs"] = np.ascontiguousarray(ktcs.reshape(16, 128, 128))
    sh["ktss"] = np.ascontiguousarray(ktss.reshape(16, 128, 128))
    sh["iotc"] = np.ascontiguousarray(
        np.arange(2048, dtype=np.float32).reshape(16, 128).T)
    perm = np.zeros((128, 128), np.float32)
    for i in range(64):
        perm[i, i + 64] = 1.0
        perm[i + 64, i] = 1.0
    sh["perm"] = perm
    sh["lutB"] = np.ascontiguousarray(
        np.tile(inputs["L1_ffn_lut"].reshape(1, 16).astype(np.float32), (128, 1)))
    return sh


def _core_idx_shards(inputs):
    gT = inputs["L1_ffn_gate_idx"].T.reshape(16, 128, 5504).astype(np.int32)
    uT = inputs["L1_ffn_up_idx"].T.reshape(16, 128, 5504).astype(np.int32)
    dT = (inputs["L1_ffn_down_idx"].reshape(16, 128, 43, 128)
          .transpose(0, 3, 2, 1).reshape(16, 128, 5504).astype(np.int32))
    out = []
    for r in range(8):
        out.append(np.ascontiguousarray(np.stack(
            [gT[2 * r], gT[2 * r + 1], uT[2 * r], uT[2 * r + 1],
             dT[2 * r], dT[2 * r + 1]])))
    return out


def kernel(**inputs):
    X = np.ascontiguousarray(inputs["L0_attn_x"], dtype=np.float32)
    sh = _prep_inputs(inputs)
    shards = _core_idx_shards(inputs)
    nc = build(n_phases=4)
    in_maps = []
    for c in range(8):
        m = dict(sh)
        m["x"] = np.ascontiguousarray(X[c])
        m["myIdx"] = shards[c]
        in_maps.append(m)
    res = run_bass_kernel_spmd(nc, in_maps, core_ids=list(range(8)))
    return np.stack([res.results[c]["y"] for c in range(8)], 0)
